# version 08 (final)
# baseline (speedup 1.0000x reference)
import numpy as np
import ml_dtypes
from contextlib import ExitStack
import concourse.bass as bass
import concourse.mybir as mybir
from concourse.bass_utils import run_bass_kernel_spmd

F32 = mybir.dt.float32
BF16 = mybir.dt.bfloat16
I32 = mybir.dt.int32
AF = mybir.ActivationFunctionType
ALU = mybir.AluOpType
AX = mybir.AxisListType

ENG_BLOCK = {"pe": "tensor", "dve": "vector", "act": "scalar", "pool": "gpsimd", "sp": "sync"}
COMPUTE_ENGS = ("pe", "dve", "act", "pool")

D = 1024
T = 2048
NBT = 32
NCORES = 8
NB = NBT // NCORES
DI = 2048
HN = 32
HP = 64
DS = 32
CH = 64
F0 = 3072
DFF = 2048
NE = 8
S127 = 1.0 / 127.0


class TT:
    __slots__ = ("name", "w", "readers")

    def __init__(self, name):
        self.name = name
        self.w = {}
        self.readers = {}

    def merge_from(self, other):
        for s, c in other.w.items():
            if self.w.get(s, 0) < c:
                self.w[s] = c
        for s, c in other.readers.items():
            if self.readers.get(s, 0) < c:
                self.readers[s] = c


class Chan:
    __slots__ = ("name", "cnt")

    def __init__(self, name):
        self.name = name
        self.cnt = 0


class K:
    def __init__(self, nc, stack):
        self.nc = nc
        self.stack = stack
        self.streams = {e: [] for e in ENG_BLOCK}
        self.vc = {e: {} for e in ENG_BLOCK}
        self.sems = {}
        self.cnt = {}
        for e in COMPUTE_ENGS:
            self._mk_sem("S" + e)
            self.cnt["S" + e] = 0
        self.chans = {}
        self.n_inst = 0

    def _mk_sem(self, name):
        self.sems[name] = self.stack.enter_context(self.nc.semaphore(name))

    def chan(self, name):
        if name not in self.chans:
            self._mk_sem(name)
            self.chans[name] = Chan(name)
        return self.chans[name]

    def _waits(self, eng, reads, writes, prune=True):
        need = {}
        for t in reads:
            for s, c in t.w.items():
                if need.get(s, 0) < c:
                    need[s] = c
        for t in writes:
            for s, c in t.w.items():
                if need.get(s, 0) < c:
                    need[s] = c
            for s, c in t.readers.items():
                if need.get(s, 0) < c:
                    need[s] = c
        vc = self.vc[eng]
        for s in sorted(need):
            c = need[s]
            if vc.get(s, 0) < c or (not prune and need[s] > 0):
                if vc.get(s, 0) < c:
                    vc[s] = c
                h = self.sems[s]
                self.streams[eng].append(lambda e, h=h, c=c: e.wait_ge(h, c))
                self.n_inst += 1

    def op(self, eng, emit, reads=(), writes=()):
        self._waits(eng, reads, writes)
        sname = "S" + eng
        self.cnt[sname] += 1
        cafter = self.cnt[sname]
        h = self.sems[sname]
        def run(e, emit=emit, h=h):
            emit(e).then_inc(h, 1)
        self.streams[eng].append(run)
        self.n_inst += 1
        for t in writes:
            if t.w.get(sname, 0) < cafter:
                t.w[sname] = cafter
        for t in reads:
            if t.readers.get(sname, 0) < cafter:
                t.readers[sname] = cafter

    def ops(self, eng, emits, reads=(), writes=()):
        self._waits(eng, reads, writes)
        sname = "S" + eng
        self.cnt[sname] += 1
        cafter = self.cnt[sname]
        h = self.sems[sname]
        for em in emits[:-1]:
            self.streams[eng].append(lambda e, em=em: em(e))
        def run(e, em=emits[-1], h=h):
            em(e).then_inc(h, 1)
        self.streams[eng].append(run)
        self.n_inst += len(emits)
        for t in writes:
            if t.w.get(sname, 0) < cafter:
                t.w[sname] = cafter
        for t in reads:
            if t.readers.get(sname, 0) < cafter:
                t.readers[sname] = cafter

    def dma(self, queue, chan, out_ap, in_ap, reads=(), writes=(), **kw):
        self._waits(queue, reads, writes, prune=False)
        chan.cnt += 16
        cafter = chan.cnt
        h = self.sems[chan.name]
        def run(e, out_ap=out_ap, in_ap=in_ap, h=h, kw=kw):
            e.dma_start(out=out_ap, in_=in_ap, **kw).then_inc(h, 16)
        self.streams[queue].append(run)
        self.n_inst += 1
        for t in writes:
            if t.w.get(chan.name, 0) < cafter:
                t.w[chan.name] = cafter
        for t in reads:
            if t.readers.get(chan.name, 0) < cafter:
                t.readers[chan.name] = cafter

    def barrier_wait(self, eng, tts):
        self._waits(eng, tts, ())

    def build(self, block):
        for eng, attr in ENG_BLOCK.items():
            stream = self.streams[eng]
            if not stream:
                continue
            getattr(block, attr)(lambda e, stream=stream: [f(e) for f in stream])


class Reg:
    __slots__ = ("t", "r0", "nr", "c0", "ncols", "tt")

    def __init__(self, t, r0, nr, c0, ncols, tt):
        self.t, self.r0, self.nr, self.c0, self.ncols = t, r0, nr, c0, ncols
        self.tt = tt

    def ap(self, rs=None, cs=None):
        r0, r1 = (0, self.nr) if rs is None else rs
        c0, c1 = (0, self.ncols) if cs is None else cs
        assert 0 <= r0 < r1 <= self.nr and 0 <= c0 < c1 <= self.ncols, \
            (rs, cs, self.nr, self.ncols)
        return self.t[self.r0 + r0:self.r0 + r1, self.c0 + c0:self.c0 + c1]


class Arena:
    def __init__(self, K_, name, cols, dtype):
        self.K = K_
        self.name = name
        self.t = K_.stack.enter_context(K_.nc.sbuf_tensor(name, [128, cols], dtype))
        self.cols = cols
        self.live = {}
        self.legacy = TT(name + "_legacy")

    def carve(self, name, rows, cols, r0=0, c0=None):
        assert c0 is not None
        if name in self.live:
            r = self.live[name]
            assert (r.r0, r.nr, r.c0, r.ncols) == (r0, rows, c0, cols), name
            return r
        assert c0 + cols <= self.cols, (self.name, name, c0, cols, self.cols)
        tt = TT(self.name + "." + name)
        tt.merge_from(self.legacy)
        r = Reg(self.t, r0, rows, c0, cols, tt)
        self.live[name] = r
        return r

    def alias(self, reg, r0, rows, c0, cols):
        return Reg(reg.t, r0, rows, reg.c0 + c0, cols, reg.tt)

    def new_phase(self):
        for r in self.live.values():
            self.legacy.merge_from(r.tt)
        self.live = {}



def _prep_inputs(inputs):
    f = np.ascontiguousarray
    sh = {}
    sh["w0_in"] = f(inputs["L0_attn_W_in"].T)
    sh["w0_a"] = f(inputs["L0_attn_W_alpha"].T)
    sh["w0_b"] = f(inputs["L0_attn_W_beta"].T)
    sh["w0_v"] = f(inputs["L0_attn_W_v"].T)
    sh["w0_o"] = f(inputs["L0_attn_W_out"].T)
    sh["b0_a"] = f(inputs["L0_attn_b_alpha"].reshape(8, 128).T)
    sh["b0_b"] = f(inputs["L0_attn_b_beta"].reshape(8, 128).T)
    sh["b0_v"] = f(inputs["L0_attn_b_v"].reshape(8, 128).T)
    sh["wg0"] = f(inputs["L0_ffn_w_gate"].T)
    sh["wu0"] = f(inputs["L0_ffn_w_up"].T)
    sh["wd0"] = f(inputs["L0_ffn_w_down"].T)
    W1 = np.array(inputs["L1_attn_W_in"])
    perm = np.concatenate([np.arange(0, 16, 2), np.arange(1, 16, 2), np.arange(16, 32)])
    Bcols = 2 * DI + perm
    Ccols = 2 * DI + DS + perm
    W1p = W1.copy()
    W1p[:, 2 * DI:2 * DI + DS] = W1[:, Bcols]
    W1p[:, 2 * DI + DS:2 * DI + 2 * DS] = W1[:, Ccols]
    sh["w1_in"] = f(W1p)
    sh["ln1_w"] = f(inputs["L1_attn_ln_w"].reshape(8, 128).T)
    sh["ln1_b"] = f(inputs["L1_attn_ln_b"].reshape(8, 128).T)
    sh["dt_bias"] = f(inputs["L1_attn_dt_bias"].reshape(-1, 1))
    sh["a_log"] = f(inputs["L1_attn_A_log"].reshape(-1, 1))
    sh["d_row"] = f(inputs["L1_attn_D"].reshape(1, -1))
    sh["rms_w"] = f(inputs["L1_attn_rms_w"].reshape(16, 128).T)
    sh["w1_out"] = f(inputs["L1_attn_W_out"])
    sh["rtr"] = f(inputs["L1_ffn_router_w"].T.reshape(8, 128, NE)
                  .transpose(1, 0, 2).reshape(128, 8 * NE))
    sh["mgw"] = f(inputs["L1_ffn_gate_w"])
    sh["mgb"] = f(inputs["L1_ffn_gate_b"].reshape(NE, 16, 128)
                  .transpose(2, 0, 1).reshape(128, NE * 16))
    sh["muw"] = f(inputs["L1_ffn_up_w"])
    sh["mub"] = f(inputs["L1_ffn_up_b"].reshape(NE, 16, 128)
                  .transpose(2, 0, 1).reshape(128, NE * 16))
    sh["mdw"] = f(inputs["L1_ffn_down_w"])
    sh["mdb"] = f(inputs["L1_ffn_down_b"])
    sh["ident"] = np.eye(128, dtype=np.float32)
    Tt = inputs["L0_attn_x"].shape[1]
    t = np.arange(Tt, dtype=np.float64)
    inv_freq = 1.0 / 10000.0 ** (np.arange(0, 16, 2, dtype=np.float64) / 16.0)
    ang = inv_freq[:, None] * t[None, :]
    cos = np.cos(ang).astype(np.float32)
    sin = np.sin(ang).astype(np.float32)
    cs1h = np.ones((32, Tt), np.float32)
    cs2h = np.zeros((32, Tt), np.float32)
    cs1h[0:8] = cos; cs1h[8:16] = cos
    cs2h[0:8] = -sin; cs2h[8:16] = sin
    sh["cs1"] = f(np.concatenate([cs1h, cs1h], 0))
    sh["cs2"] = f(np.concatenate([cs2h, cs2h], 0))
    swp = np.zeros((64, 64), np.float32)
    for blk in (0, 32):
        for i in range(8):
            swp[blk + 8 + i, blk + i] = 1.0
            swp[blk + i, blk + 8 + i] = 1.0
        for i in range(16, 32):
            swp[blk + i, blk + i] = 1.0
    sh["swap64"] = swp
    sel64 = np.zeros((HN, HN * 64), np.float32)
    for h in range(HN):
        sel64[h, h * 64:(h + 1) * 64] = 1.0
    sh["sel64"] = sel64
    ut = np.triu(np.ones((64, 64), np.float32))
    sh["utmask"] = f(np.concatenate([ut, ut], 0))
    sh["ident64b"] = np.eye(64, dtype=np.float32)
    sh["ones128"] = np.ones((1, 128), np.float32)
    sh["onescol"] = np.ones((128, 1), np.float32)
    x = inputs["L0_attn_x"]
    xfm = f(np.transpose(x, (0, 2, 1)))
    nbt = x.shape[0]
    nbc = nbt // NCORES
    per_core = [f(xfm[c * nbc:(c + 1) * nbc]) for c in range(NCORES)]
    return sh, per_core



def build_core(dev=False, stages=3, Tdev=2048, TMcap=1024):
    global T
    T = Tdev
    nc = bass.Bass("TRN2", target_bir_lowering=False, debug=False)
    stack = ExitStack()
    k = K(nc, stack)
    NTt = min(512, T)
    SL = min(512, T)
    TM = min(TMcap, T)
    NJ = T // NTt
    NS = T // SL
    NCP = SL // 128
    DI_ = DI

    x_in = nc.dram_tensor("x", [NB, D, T], F32, kind="ExternalInput").ap()
    out = nc.dram_tensor("out", [NB, T, D], F32, kind="ExternalOutput").ap()

    def win(name, shape, dt=F32):
        return nc.dram_tensor(name, list(shape), dt, kind="ExternalInput").ap()
    w0_in = win("w0_in", [D, D]); w0_a = win("w0_a", [D, D]); w0_b = win("w0_b", [D, D])
    w0_v = win("w0_v", [D, D]); w0_o = win("w0_o", [D, D])
    b0_a = win("b0_a", [128, 8]); b0_b = win("b0_b", [128, 8]); b0_v = win("b0_v", [128, 8])
    wg0 = win("wg0", [D, F0], I32); wu0 = win("wu0", [D, F0], I32)
    wd0 = win("wd0", [F0, D], I32)
    w1_in = win("w1_in", [D, 4192]); ln1_w = win("ln1_w", [128, 8]); ln1_b = win("ln1_b", [128, 8])
    dt_bias = win("dt_bias", [HN, 1]); a_log = win("a_log", [HN, 1])
    d_row = win("d_row", [1, HN])
    rms_w = win("rms_w", [128, 16]); w1_out = win("w1_out", [DI_, D])
    rtr = win("rtr", [128, 8 * NE])
    mgw = win("mgw", [NE, D, DFF]); mgb = win("mgb", [128, NE * 16])
    muw = win("muw", [NE, D, DFF]); mub = win("mub", [128, NE * 16])
    mdw = win("mdw", [NE, DFF, D]); mdb = win("mdb", [NE, D])
    ident = win("ident", [128, 128]); cs1 = win("cs1", [64, T]); cs2 = win("cs2", [64, T])
    swap64 = win("swap64", [64, 64])
    sel64 = win("sel64", [HN, HN * 64]); utmask = win("utmask", [128, 64])
    ident64b = win("ident64b", [64, 64]); ones128 = win("ones128", [1, 128])
    onescol = win("onescol", [128, 1])
    skind = "ExternalOutput" if dev else "Internal"
    s0_d = nc.dram_tensor("s0_d", [NB, D, T], F32, kind=skind).ap()
    x1_d = nc.dram_tensor("x1_d", [NB, D, T], F32, kind=skind).ap()
    s1_d = nc.dram_tensor("s1_d", [NB, D, T], F32, kind=skind).ap()
    if dev:
        xn_dbg = nc.dram_tensor("xn_dbg", [NB, D, T], F32, kind="ExternalOutput").ap()
        y_dbg = nc.dram_tensor("y_dbg", [NB, DI_, T], F32, kind="ExternalOutput").ap()
        g1_dbg = nc.dram_tensor("g1_dbg", [NB, D, T], F32, kind="ExternalOutput").ap()
        ga_dbg = nc.dram_tensor("ga_dbg", [NB, T, D], F32, kind="ExternalOutput").ap()
        p_dbg = nc.dram_tensor("p_dbg", [NB, T, NE], F32, kind="ExternalOutput").ap()

    XN = Arena(k, "XN", 8 * T, BF16)
    AB = Arena(k, "AB", max(8 * T, 16 * TM, 32 * NTt, 24 * SL + DI + 1024), BF16)
    CB = Arena(k, "CB", max(8 * T, 4 * DI_ + 16 * SL, 8 * TM), BF16)
    WB = Arena(k, "WB", 8192, BF16)
    FA = Arena(k, "FA", max(4 * T, 16 * NTt, 16 * SL, 8 * TM), F32)
    SM = Arena(k, "SM", 7040, F32)
    CSTF = Arena(k, "CSTF", 2436, F32)
    CSTB = Arena(k, "CSTB", 4292, BF16)
    psum = [stack.enter_context(nc.psum_tensor(f"pb{i}", [128, 512], F32)) for i in range(8)]
    ptt = [TT(f"pb{i}") for i in range(8)]

    c_ld = k.chan("c_ld")
    c_ldb = k.chan("c_ldb")
    R_id = CSTF.carve("ident", 128, 128, c0=0)
    R_swap = CSTF.carve("swap", 64, 64, c0=128)
    R_ut = CSTF.carve("ut", 128, 64, c0=192)
    R_sel64 = CSTF.carve("sel64", HN, HN * 64, c0=256)
    R_ones = CSTF.carve("ones", 1, 128, c0=256 + HN * 64)
    R_onec = CSTF.carve("onec", 128, 1, c0=384 + HN * 64)
    assert 385 + HN * 64 <= 2436
    k.dma("sp", c_ld, R_id.ap(), ident, writes=[R_id.tt])
    k.dma("sp", c_ld, R_swap.ap(), swap64, writes=[R_swap.tt])
    k.dma("sp", c_ld, R_ut.ap(), utmask, writes=[R_ut.tt])
    k.dma("sp", c_ld, R_sel64.ap(), sel64, writes=[R_sel64.tt])
    k.dma("sp", c_ld, R_ones.ap(), ones128, writes=[R_ones.tt])
    k.dma("sp", c_ld, R_onec.ap(), onescol, writes=[R_onec.tt])
    R_rtrb = CSTB.carve("rtrb", 128, 8 * NE, c0=0)
    R_onesb = CSTB.carve("onesb", 1, 128, c0=64)
    R_id64b = CSTB.carve("id64b", 64, 64, c0=192)
    R_dident = CSTB.carve("dident", 128, HN * 64, c0=256)
    R_mt = [CSTB.carve(f"mt{i}", 128, 64, c0=256 + HN * 64 + i * 64) for i in range(2)]
    R_bkd = [CSTB.carve(f"bkd{i}", 128, 32, c0=384 + HN * 64 + i * 32) for i in range(2)]
    R_csA = CSTB.carve("csA", 128, SL, c0=448 + HN * 64)
    R_csB = CSTB.carve("csB", 32, SL, c0=448 + HN * 64 + SL)
    R_onecb = CSTB.carve("onecb", 128, 1, c0=448 + HN * 64 + 2 * SL)
    assert 449 + HN * 64 + 2 * SL <= 3524
    k.dma("pool", c_ldb, R_rtrb.ap(), rtr, writes=[R_rtrb.tt])
    k.dma("pool", c_ldb, R_onesb.ap(), ones128, writes=[R_onesb.tt])
    k.dma("pool", c_ldb, R_onecb.ap(), onescol, writes=[R_onecb.tt])
    k.dma("pool", c_ldb, R_id64b.ap(), ident64b, writes=[R_id64b.tt])
    smp = [6144]
    def smpark(name, rows, cols, r0=0):
        r = SM.carve(name, rows, cols, r0=r0, c0=smp[0])
        smp[0] += cols
        assert smp[0] <= 7040, smp[0]
        return r
    R_b0a = smpark("b0a", 128, 8); R_b0b = smpark("b0b", 128, 8); R_b0v = smpark("b0v", 128, 8)
    R_ln1w = smpark("ln1w", 128, 8); R_ln1b = smpark("ln1b", 128, 8)
    R_rmsw = smpark("rmsw", 128, 16)
    R_dtb = smpark("dtb", HN, 1, r0=64); R_alog = smpark("alog", HN, 1, r0=64)
    R_acol = smpark("acol", HN, 1, r0=64)
    R_d64 = smpark("d64", 64, HN)
    R_drow = smpark("drow", 1, HN)
    R_mgb = smpark("mgb", 128, NE * 16); R_mub = smpark("mub", 128, NE * 16)
    for src, reg in ((b0_a, R_b0a), (b0_b, R_b0b), (b0_v, R_b0v),
                     (ln1_w, R_ln1w), (ln1_b, R_ln1b)):
        k.dma("sp", c_ld, reg.ap(), src, writes=[reg.tt])
    k.dma("sp", c_ld, R_rmsw.ap(), rms_w, writes=[R_rmsw.tt])
    k.dma("sp", c_ld, R_dtb.ap(), dt_bias, writes=[R_dtb.tt])
    k.dma("sp", c_ld, R_alog.ap(), a_log, writes=[R_alog.tt])
    k.dma("sp", c_ld, R_drow.ap(), d_row, writes=[R_drow.tt])
    k.dma("sp", c_ld, R_mgb.ap(), mgb, writes=[R_mgb.tt])
    k.dma("sp", c_ld, R_mub.ap(), mub, writes=[R_mub.tt])
    R_mdbblk = SM.carve("mdbblk", NE, D, r0=64, c0=512)
    k.dma("sp", c_ld, R_mdbblk.ap(), mdb, writes=[R_mdbblk.tt])
    for _reg in (R_id, R_swap, R_ut, R_sel64, R_ones, R_onec, R_b0a, R_b0b, R_b0v,
                 R_ln1w, R_ln1b, R_rmsw, R_dtb, R_alog, R_drow, R_mgb, R_mub,
                 R_mdbblk):
        _reg.tt.w["c_ld"] = c_ld.cnt
    for _reg in (R_rtrb, R_onesb, R_onecb, R_id64b):
        _reg.tt.w["c_ldb"] = c_ldb.cnt

    k.op("act", lambda e: e.activation(out=R_acol.ap(), in_=R_alog.ap(), func=AF.Exp),
         reads=[R_alog.tt], writes=[R_acol.tt])
    k.op("dve", lambda e: e.tensor_scalar_mul(R_acol.ap(), R_acol.ap(), -1.0),
         reads=[R_acol.tt], writes=[R_acol.tt])
    k.ops("pe", [lambda e: e.matmul(psum[7][0:64, 0:HN], R_ones.ap((0, 1), (0, 64)),
                                    R_drow.ap(), start=True, stop=True)],
          reads=[R_ones.tt, R_drow.tt], writes=[ptt[7]])
    k.op("dve", lambda e: e.tensor_copy(R_d64.ap(), psum[7][0:64, 0:HN]),
         reads=[ptt[7]], writes=[R_d64.tt])
    for h in range(HN):
        k.op("dve", lambda e, h=h: e.tensor_scalar(
            R_dident.ap((0, 64), (h * 64, h * 64 + 64)), R_id64b.ap(),
            R_d64.ap((0, 64), (h, h + 1)), None, ALU.mult),
            reads=[R_id64b.tt, R_d64.tt], writes=[R_dident.tt])
        k.op("dve", lambda e, h=h: e.tensor_copy(
            R_dident.ap((64, 128), (h * 64, h * 64 + 64)),
            R_dident.ap((0, 64), (h * 64, h * 64 + 64))),
            reads=[R_dident.tt], writes=[R_dident.tt])

    def ln_fm(src_regs, src_aps, dst_ap_fn, dst_tts, nt, affine=None, eps=1e-5,
              ones_x=None):
        if ones_x is None:
            ones_x = R_onec
        pS, pQ = psum[6], psum[7]
        R_sq = SM.carve("lnsq", 128, 512, c0=0)
        mean = SM.carve("ln_mean", 1, 512, c0=512)
        rstd = SM.carve("ln_rstd", 1, 512, c0=1024)
        msq = SM.carve("ln_msq", 1, 512, c0=1536)
        var = SM.carve("ln_var", 1, 512, c0=2048)
        ems = []
        for c in range(8):
            ems.append(lambda e, c=c: e.matmul(pS[0:1, 0:nt], ones_x.ap(),
                                               src_aps[c], start=(c == 0), stop=(c == 7)))
        k.ops("pe", ems, reads=[ones_x.tt] + [r.tt for r in src_regs], writes=[ptt[6]])
        for c in range(8):
            k.op("act", lambda e, c=c: e.activation(out=R_sq.ap((0, 128), (0, nt)),
                 in_=src_aps[c], func=AF.Square),
                 reads=[src_regs[c].tt], writes=[R_sq.tt])
            k.ops("pe", [lambda e, c=c: e.matmul(pQ[0:1, 0:nt], R_onec.ap(),
                         R_sq.ap((0, 128), (0, nt)), start=(c == 0), stop=(c == 7))],
                  reads=[R_onec.tt, R_sq.tt], writes=[ptt[7]])
        k.op("dve", lambda e: e.tensor_scalar_mul(mean.ap((0, 1), (0, nt)), pS[0:1, 0:nt],
             1.0 / D), reads=[ptt[6]], writes=[mean.tt])
        k.op("act", lambda e: e.activation(out=msq.ap((0, 1), (0, nt)),
             in_=mean.ap((0, 1), (0, nt)), func=AF.Square), reads=[mean.tt], writes=[msq.tt])
        k.op("dve", lambda e: e.scalar_tensor_tensor(var.ap((0, 1), (0, nt)), pQ[0:1, 0:nt],
             1.0 / D, msq.ap((0, 1), (0, nt)), ALU.mult, ALU.subtract),
             reads=[ptt[7], msq.tt], writes=[var.tt])
        k.op("dve", lambda e: e.tensor_scalar_add(var.ap((0, 1), (0, nt)),
             var.ap((0, 1), (0, nt)), float(eps)), reads=[var.tt], writes=[var.tt])
        k.op("act", lambda e: e.activation(out=msq.ap((0, 1), (0, nt)),
             in_=var.ap((0, 1), (0, nt)), func=AF.Ln),
             reads=[var.tt], writes=[msq.tt])
        k.op("act", lambda e: e.activation(out=rstd.ap((0, 1), (0, nt)),
             in_=msq.ap((0, 1), (0, nt)), func=AF.Exp, scale=-0.5),
             reads=[msq.tt], writes=[rstd.tt])
        k.ops("pe", [lambda e: e.matmul(pS[0:128, 0:nt], R_ones.ap(),
                     mean.ap((0, 1), (0, nt)), start=True, stop=True)],
              reads=[R_ones.tt, mean.tt], writes=[ptt[6]])
        k.ops("pe", [lambda e: e.matmul(pQ[0:128, 0:nt], R_ones.ap(),
                     rstd.ap((0, 1), (0, nt)), start=True, stop=True)],
              reads=[R_ones.tt, rstd.tt], writes=[ptt[7]])
        R_t = R_sq
        for c in range(8):
            k.op("dve", lambda e, c=c: e.tensor_sub(R_t.ap((0, 128), (0, nt)), src_aps[c],
                 pS[0:128, 0:nt]), reads=[src_regs[c].tt, ptt[6]], writes=[R_t.tt])
            if affine is None:
                k.op("dve", lambda e, c=c: e.tensor_mul(dst_ap_fn(c),
                     R_t.ap((0, 128), (0, nt)), pQ[0:128, 0:nt]),
                     reads=[R_t.tt, ptt[7]], writes=[dst_tts[c]])
            else:
                aw, ab_ = affine
                k.op("dve", lambda e, c=c: e.tensor_mul(R_t.ap((0, 128), (0, nt)),
                     R_t.ap((0, 128), (0, nt)), pQ[0:128, 0:nt]),
                     reads=[R_t.tt, ptt[7]], writes=[R_t.tt])
                k.op("dve", lambda e, c=c: e.tensor_scalar(dst_ap_fn(c),
                     R_t.ap((0, 128), (0, nt)), aw.ap((0, 128), (c, c + 1)),
                     ab_.ap((0, 128), (c, c + 1)), ALU.mult, ALU.add),
                     reads=[R_t.tt, aw.tt, ab_.tt], writes=[dst_tts[c]])

    def wgroup(pairs):
        ch = k.chan("ch_wb")
        for slot, src_ap in pairs:
            k.dma("pool", ch, slot.ap(), src_ap, writes=[slot.tt])
        for slot, _ in pairs:
            slot.tt.w["ch_wb"] = ch.cnt

    wseq = [0]

    def wslots(width, chunks):
        par = wseq[0] % 2
        wseq[0] += 1
        reg = WB.carve(f"wbp{par}", 128, 4096, c0=par * 4096)
        ch = k.chan(f"ch_wb{par}")
        off = 0
        slots = []
        for src2d, n in chunks:
            k.dma("pool", ch,
                  reg.ap((0, 128), (off, off + n * width)).rearrange(
                      "p (kk c) -> p kk c", c=width),
                  src2d.rearrange("(kk p) c -> p kk c", p=128),
                  writes=[reg.tt])
            for i in range(n):
                slots.append(Reg(reg.t, 0, 128, reg.c0 + off + i * width, width, reg.tt))
            off += n * width
        assert off <= 4096
        reg.tt.w[ch.name] = ch.cnt
        return slots

    def l0(b):
        for A in (XN, AB, CB, WB, FA, SM):
            A.new_phase()
        R_xn = XN.carve("xn", 128, 8 * T, c0=0)
        xnap = lambda c, t0, t1: R_xn.ap((0, 128), (c * T + t0, c * T + t1))
        R_lnx = FA.carve("lnx", 128, 8 * NTt, c0=0)
        ch_x = k.chan("ch_x")
        for j in range(NJ):
            t0 = j * NTt
            k.dma("sp", ch_x, R_lnx.ap().rearrange("p (c t) -> p c t", c=8),
                  x_in[b, :, t0:t0 + NTt].rearrange("(c p) t -> p c t", p=128),
                  writes=[R_lnx.tt])
            ln_fm([R_lnx] * 8,
                  [R_lnx.ap((0, 128), (c * NTt, (c + 1) * NTt)) for c in range(8)],
                  lambda c, t0=t0: xnap(c, t0, t0 + NTt), [R_xn.tt] * 8, NTt)
        if dev:
            ch_dbg = k.chan("ch_dbg")
            for c in range(8):
                k.dma("pool", ch_dbg, xn_dbg[b, c * 128:(c + 1) * 128, :], xnap(c, 0, T),
                      reads=[R_xn.tt])
        AB.new_phase()
        R_xp = AB.carve("xp", 128, 8 * T, c0=0)
        xpap = lambda c, t0, t1: R_xp.ap((0, 128), (c * T + t0, c * T + t1))
        for cg in range(2):
            slots = wslots(512, [(w0_in[:, cg * 512:(cg + 1) * 512], 8)])
            for j in range(NJ):
                t0 = j * NTt
                for mi in range(4):
                    m = cg * 4 + mi
                    bank = (cg * 4 + mi) % 6
                    k.ops("pe", [
                        (lambda e, kk=kk, mi=mi, bank=bank, t0=t0, slots=slots: e.matmul(
                            psum[bank][:, 0:NTt],
                            slots[kk].ap((0, 128), (mi * 128, (mi + 1) * 128)),
                            xnap(kk, t0, t0 + NTt), start=(kk == 0), stop=(kk == 7)))
                        for kk in range(8)],
                        reads=[s.tt for s in slots] + [R_xn.tt], writes=[ptt[bank]])
                    k.op("act", lambda e, m=m, bank=bank, t0=t0: e.activation(
                        out=xpap(m, t0, t0 + NTt), in_=psum[bank][:, 0:NTt], func=AF.Silu),
                        reads=[ptt[bank]], writes=[R_xp.tt])
        WB.new_phase(); FA.new_phase(); CB.new_phase()
        R_cell = CB.carve("cell", 128, 8 * T, c0=0)
        cellap = lambda c, t0, t1: R_cell.ap((0, 128), (c * T + t0, c * T + t1))
        R_al = FA.carve("al", 128, T, c0=0)
        R_bvr = FA.carve("bvr", 128, T, c0=T)
        R_bt = FA.carve("bt", 128, T, c0=2 * T)
        R_vt = FA.carve("vt", 128, T, c0=3 * T)
        for m in range(8):
            f0c = m * 128
            sl24 = wslots(128, [(w0_a[:, f0c:f0c + 128], 8),
                                (w0_b[:, f0c:f0c + 128], 8),
                                (w0_v[:, f0c:f0c + 128], 8)])
            wsl = {}
            for wi, wnm in enumerate(("a", "b", "v")):
                for kk in range(8):
                    wsl[(wnm, kk)] = sl24[wi * 8 + kk]
            if True:
                mc = 0
                for (wnm, bias_reg, dst, fn) in (("a", R_b0a, R_al, AF.Sigmoid),
                                                 ("b", R_b0b, R_bt, AF.Sigmoid),
                                                 ("v", R_b0v, R_vt, AF.Tanh)):
                    bank = {"a": 0, "b": 1, "v": 2}[wnm]
                    for j in range(NJ):
                        t0 = j * NTt
                        k.ops("pe", [
                            (lambda e, kk=kk, bank=bank, mc=mc, t0=t0, wnm=wnm, wsl=wsl: e.matmul(
                                psum[bank][:, 0:NTt],
                                wsl[(wnm, kk)].ap((0, 128), (mc, mc + 128)),
                                xpap(kk, t0, t0 + NTt), start=(kk == 0), stop=(kk == 7)))
                            for kk in range(8)],
                            reads=[wsl[(wnm, kk)].tt for kk in range(8)] + [R_xp.tt],
                            writes=[ptt[bank]])
                        k.op("act", lambda e, bank=bank, dst=dst, t0=t0, fn=fn, m=m,
                             br=bias_reg: e.activation(
                                out=dst.ap((0, 128), (t0, t0 + NTt)),
                                in_=psum[bank][:, 0:NTt], func=fn,
                                bias=br.ap((0, 128), (m, m + 1))),
                             reads=[ptt[bank], bias_reg.tt], writes=[dst.tt])
                k.op("dve", lambda e: e.tensor_mul(R_bvr.ap(), R_bt.ap(), R_vt.ap()),
                     reads=[R_bt.tt, R_vt.tt], writes=[R_bvr.tt])
                k.op("dve", lambda e: e.tensor_tensor_scan(R_bt.ap(), R_al.ap(),
                     R_bvr.ap(), 0.0, ALU.mult, ALU.add),
                     reads=[R_al.tt, R_bvr.tt], writes=[R_bt.tt])
                k.op("dve", lambda e, m=m: e.tensor_copy(cellap(m, 0, T), R_bt.ap()),
                     reads=[R_bt.tt], writes=[R_cell.tt])
        for m in range(8):
            k.op("act", lambda e, m=m: e.activation(out=R_vt.ap(), in_=cellap(m, 0, T),
                 func=AF.Silu), reads=[R_cell.tt], writes=[R_vt.tt])
            k.op("dve", lambda e, m=m: e.tensor_mul(cellap(m, 0, T), cellap(m, 0, T),
                 R_vt.ap()), reads=[R_cell.tt, R_vt.tt], writes=[R_cell.tt])
        WB.new_phase(); FA.new_phase()
        R_st0 = FA.carve("s0st", 128, 2 * NTt, c0=0)
        ch_s0 = k.chan("ch_s0")
        for cg in range(2):
            oslots = wslots(512, [(w0_o[:, cg * 512:(cg + 1) * 512], 8)])
            for mi in range(4):
                m = cg * 4 + mi
                for j in range(NJ):
                    t0 = j * NTt
                    bank = (cg * 4 + mi + j) % 6
                    sl_ = (j % 2) * NTt
                    k.ops("pe", [
                        (lambda e, kk=kk, mi=mi, bank=bank, t0=t0, oslots=oslots: e.matmul(
                            psum[bank][:, 0:NTt],
                            oslots[kk].ap((0, 128), (mi * 128, (mi + 1) * 128)),
                            cellap(kk, t0, t0 + NTt), start=(kk == 0), stop=(kk == 7)))
                        for kk in range(8)],
                        reads=[s.tt for s in oslots] + [R_cell.tt], writes=[ptt[bank]])
                    k.op("dve", lambda e, bank=bank, sl_=sl_: e.tensor_copy(
                        R_st0.ap((0, 128), (sl_, sl_ + NTt)), psum[bank][:, 0:NTt]),
                        reads=[ptt[bank]], writes=[R_st0.tt])
                    k.dma("sp", ch_s0, s0_d[b, m * 128:(m + 1) * 128, t0:t0 + NTt],
                          R_st0.ap((0, 128), (sl_, sl_ + NTt)), reads=[R_st0.tt])
        AB.new_phase(); WB.new_phase(); FA.new_phase()
        tt_s0 = TT("s0_dram"); tt_s0.w["ch_s0"] = ch_s0.cnt
        ch_x1 = k.chan("ch_x1")
        R_hf = AB.carve("hf", 128, 24 * NTt, c0=0)
        R_g0 = AB.carve("g0", 128, 8 * NTt, c0=24 * NTt)
        R_cx = FA.carve("cx", 128, 8 * NTt, c0=0)
        R_cs0 = FA.carve("cs0", 128, 8 * NTt, c0=8 * NTt)
        for j in range(NJ):
            t0 = j * NTt
            for grp in range(6):
                gslots = wslots(512, [(wg0[:, grp * 512:(grp + 1) * 512], 8)])
                for mi in range(4):
                    m = grp * 4 + mi
                    bank = (grp * 4 + mi) % 6
                    k.ops("pe", [
                        (lambda e, kk=kk, mi=mi, t0=t0, bank=bank, gslots=gslots: e.matmul(
                            psum[bank][:, 0:NTt],
                            gslots[kk].ap((0, 128), (mi * 128, (mi + 1) * 128)),
                            xnap(kk, t0, t0 + NTt), start=(kk == 0), stop=(kk == 7)))
                        for kk in range(8)],
                        reads=[s.tt for s in gslots] + [R_xn.tt], writes=[ptt[bank]])
                    k.op("act", lambda e, m=m, bank=bank: e.activation(
                        out=R_hf.ap((0, 128), (m * NTt, (m + 1) * NTt)),
                        in_=psum[bank][:, 0:NTt], func=AF.Silu, scale=S127),
                        reads=[ptt[bank]], writes=[R_hf.tt])
                uslots = wslots(512, [(wu0[:, grp * 512:(grp + 1) * 512], 8)])
                for mi in range(4):
                    m = grp * 4 + mi
                    bank = (grp * 4 + mi) % 6
                    k.ops("pe", [
                        (lambda e, kk=kk, mi=mi, t0=t0, bank=bank, uslots=uslots: e.matmul(
                            psum[bank][:, 0:NTt],
                            uslots[kk].ap((0, 128), (mi * 128, (mi + 1) * 128)),
                            xnap(kk, t0, t0 + NTt), start=(kk == 0), stop=(kk == 7)))
                        for kk in range(8)],
                        reads=[s.tt for s in uslots] + [R_xn.tt], writes=[ptt[bank]])
                    k.op("dve", lambda e, m=m, bank=bank: e.scalar_tensor_tensor(
                        R_hf.ap((0, 128), (m * NTt, (m + 1) * NTt)), psum[bank][:, 0:NTt],
                        S127, R_hf.ap((0, 128), (m * NTt, (m + 1) * NTt)),
                        ALU.mult, ALU.mult),
                        reads=[ptt[bank], R_hf.tt], writes=[R_hf.tt])
            for dgrp in range(2):
                for kh in range(4):
                    dslots = wslots(512, [(wd0[kh * 768:(kh + 1) * 768,
                                               dgrp * 512:(dgrp + 1) * 512], 6)])
                    for mo in range(4):
                        k.ops("pe", [
                            (lambda e, kk=kk, kh=kh, mo=mo, dslots=dslots: e.matmul(
                                psum[mo][:, 0:NTt],
                                dslots[kk].ap((0, 128), (mo * 128, (mo + 1) * 128)),
                                R_hf.ap((0, 128), ((kh * 6 + kk) * NTt,
                                                   (kh * 6 + kk + 1) * NTt)),
                                start=(kh == 0 and kk == 0), stop=(kh == 3 and kk == 5)))
                            for kk in range(6)],
                            reads=[s.tt for s in dslots] + [R_hf.tt], writes=[ptt[mo]])
                for mo in range(4):
                    dmo = dgrp * 4 + mo
                    k.op("act", lambda e, mo=mo, dmo=dmo: e.activation(
                        out=R_g0.ap((0, 128), (dmo * NTt, (dmo + 1) * NTt)),
                        in_=psum[mo][:, 0:NTt], func=AF.Silu, scale=S127),
                        reads=[ptt[mo]], writes=[R_g0.tt])
            k.dma("sp", ch_x, R_cx.ap().rearrange("p (c t) -> p c t", c=8),
                  x_in[b, :, t0:t0 + NTt].rearrange("(c p) t -> p c t", p=128),
                  writes=[R_cx.tt])
            k.dma("sp", k.chan("ch_s0r"), R_cs0.ap().rearrange("p (c t) -> p c t", c=8),
                  s0_d[b, :, t0:t0 + NTt].rearrange("(c p) t -> p c t", p=128),
                  reads=[tt_s0], writes=[R_cs0.tt])
            for c in range(8):
                k.op("dve", lambda e, c=c: e.tensor_mul(
                    R_cs0.ap((0, 128), (c * NTt, (c + 1) * NTt)),
                    R_cs0.ap((0, 128), (c * NTt, (c + 1) * NTt)),
                    R_g0.ap((0, 128), (c * NTt, (c + 1) * NTt))),
                    reads=[R_cs0.tt, R_g0.tt], writes=[R_cs0.tt])
                k.op("dve", lambda e, c=c: e.tensor_add(
                    R_cx.ap((0, 128), (c * NTt, (c + 1) * NTt)),
                    R_cx.ap((0, 128), (c * NTt, (c + 1) * NTt)),
                    R_cs0.ap((0, 128), (c * NTt, (c + 1) * NTt))),
                    reads=[R_cx.tt, R_cs0.tt], writes=[R_cx.tt])
                k.dma("sp", ch_x1, x1_d[b, c * 128:(c + 1) * 128, t0:t0 + NTt],
                      R_cx.ap((0, 128), (c * NTt, (c + 1) * NTt)), reads=[R_cx.tt])
            ln_fm([R_cx] * 8,
                  [R_cx.ap((0, 128), (c * NTt, (c + 1) * NTt)) for c in range(8)],
                  lambda c, t0=t0: xnap(c, t0, t0 + NTt), [R_xn.tt] * 8, NTt)
        tt_x1 = TT("x1_dram")
        tt_x1.w["ch_x1"] = ch_x1.cnt
        return tt_x1

    def l1a(b):
        for A in (AB, CB, WB, FA, SM):
            A.new_phase()
        R_xn = XN.live["xn"]
        xnap = lambda c, t0, t1: R_xn.ap((0, 128), (c * T + t0, c * T + t1))
        R_rawbc = SM.carve("rawbc", 64, SL, r0=0, c0=2560)
        R_ropB = SM.carve("ropB", 32, SL, r0=64, c0=2560)
        R_dec = SM.carve("dec", 32, SL, r0=0, c0=3072)
        R_a = SM.carve("a", 32, SL, r0=32, c0=3072)
        R_ropC = SM.carve("ropC", 32, SL, r0=64, c0=3072)
        R_acs = SM.carve("acs", 32, SL, r0=0, c0=3584)
        R_dtm = SM.carve("dtm", 32, SL, r0=64, c0=3584)
        R_eacs = SM.carve("eacs", 32, SL, r0=0, c0=4096)
        R_dta = SM.carve("dta", 32, SL, r0=64, c0=4096)
        R_csum = SM.carve("csum", 32, 8, r0=0, c0=4608)
        R_ecs = SM.carve("ecs", 32, 8, r0=0, c0=4616)
        R_cks = [SM.carve(f"cks{i}", 32, 128, r0=32, c0=3584 + i * 128) for i in range(4)]
        R_ebc = SM.carve("ebc", 32, 256, r0=32, c0=4864)
        R_dta2 = SM.carve("dta2", 32, SL, r0=64, c0=4608)
        R_dt = SM.carve("dt", 32, SL, r0=64, c0=5120)
        R_sal = [SM.carve(f"sal{h}", HN, 64, r0=32, c0=512 + h * 64) for h in range(HN)]
        tks = 5632
        R_nacs = [SM.carve(f"nacs{i}", 128, 32, c0=tks + i * 32) for i in range(2)]
        R_dectk = [SM.carve(f"dectk{i}", 128, 32, c0=tks + 64 + i * 32) for i in range(2)]
        R_dttk = [SM.carve(f"dttk{i}", 128, 32, c0=tks + 128 + i * 32) for i in range(2)]
        R_bktk = [SM.carve(f"bktk{i}", 128, 32, c0=tks + 192 + i * 32) for i in range(2)]
        R_scm = [SM.carve(f"scm{i}", 128, 64, c0=tks + 256 + i * 64) for i in range(2)]
        R_mdf = [SM.carve(f"mdf{i}", 128, 64, c0=tks + 384 + i * 64) for i in range(2)]
        assert tks + 512 <= 6144
        R_h1 = AB.carve("h1", 128, 8 * SL, c0=0)
        R_sz = AB.carve("sz", 128, 16 * SL, c0=8 * SL)
        R_mta = AB.carve("mta", 128, DI_, c0=24 * SL)
        R_bka = AB.carve("bka", 128, 1024, c0=24 * SL + DI_)
        R_mex = [SM.carve(f"mex{i}", 128, 64, c0=6528 + i * 64) for i in range(4)]
        R_xst = CB.carve("xst", 128, 4 * DI_, c0=0)
        R_y = CB.carve("y", 128, 16 * SL, c0=4 * DI_)
        R_yg = FA.carve("yg", 128, 16 * SL, c0=0)
        NCH = SL // CH
        ch_s1 = k.chan("ch_s1")
        ch_cs = k.chan("ch_cs")
        for h in range(HN):
            k.op("dve", lambda e, h=h: e.memset(R_sal[h].ap(), 0.0), writes=[R_sal[h].tt])
        for sl in range(NS):
            u0 = sl * SL
            k.dma("pool", ch_cs, R_csA.ap((0, 32), (0, SL)), cs1[0:32, u0:u0 + SL],
                  writes=[R_csA.tt])
            k.dma("pool", ch_cs, R_csA.ap((32, 64), (0, SL)), cs1[32:64, u0:u0 + SL],
                  writes=[R_csA.tt])
            k.dma("pool", ch_cs, R_csA.ap((64, 96), (0, SL)), cs2[0:32, u0:u0 + SL],
                  writes=[R_csA.tt])
            k.dma("pool", ch_cs, R_csB.ap((0, 32), (0, SL)), cs2[32:64, u0:u0 + SL],
                  writes=[R_csB.tt])
            R_csA.tt.w["ch_cs"] = ch_cs.cnt
            R_csB.tt.w["ch_cs"] = ch_cs.cnt
            ln_fm([R_xn] * 8, [xnap(c, u0, u0 + SL) for c in range(8)],
                  lambda c: R_h1.ap((0, 128), (c * SL, (c + 1) * SL)), [R_h1.tt] * 8,
                  SL, affine=(R_ln1w, R_ln1b), eps=1e-6, ones_x=R_onecb)
            for zg in range(4):
                zsl = wslots(512, [(w1_in[:, zg * 512:(zg + 1) * 512], 8)])
                for mi in range(4):
                    m = zg * 4 + mi
                    bank = (zg * 4 + mi) % 6
                    k.ops("pe", [
                        (lambda e, kk=kk, mi=mi, bank=bank, zsl=zsl: e.matmul(
                            psum[bank][:, 0:SL],
                            zsl[kk].ap((0, 128), (mi * 128, (mi + 1) * 128)),
                            R_h1.ap((0, 128), (kk * SL, (kk + 1) * SL)),
                            start=(kk == 0), stop=(kk == 7)))
                        for kk in range(8)],
                        reads=[s.tt for s in zsl] + [R_h1.tt], writes=[ptt[bank]])
                    k.op("act", lambda e, m=m, bank=bank: e.activation(
                        out=R_sz.ap((0, 128), (m * SL, (m + 1) * SL)),
                        in_=psum[bank][:, 0:SL], func=AF.Silu),
                        reads=[ptt[bank]], writes=[R_sz.tt])
            for xg in range(4):
                xsl = wslots(512, [(w1_in[:, DI_ + xg * 512:DI_ + (xg + 1) * 512], 8)])
                for tc in range(NCP):
                    bank = 2 + (xg % 2) * 2 + (tc % 2)
                    k.ops("pe", [
                        (lambda e, kk=kk, tc=tc, bank=bank, xsl=xsl: e.matmul(
                            psum[bank][:, 0:512],
                            R_h1.ap((0, 128), (kk * SL + tc * 128,
                                               kk * SL + (tc + 1) * 128)),
                            xsl[kk].ap(), start=(kk == 0), stop=(kk == 7)))
                        for kk in range(8)],
                        reads=[s.tt for s in xsl] + [R_h1.tt], writes=[ptt[bank]])
                    k.op("act", lambda e, tc=tc, xg=xg, bank=bank: e.activation(
                        out=R_xst.ap((0, 128), (tc * DI_ + xg * 512,
                                                tc * DI_ + (xg + 1) * 512)),
                        in_=psum[bank][:, 0:512], func=AF.Silu),
                        reads=[ptt[bank]], writes=[R_xst.tt])
            bsl = [CSTB.carve(f"w1b{kk}", 128, 96, c0=3524 + kk * 96) for kk in range(8)]
            wgroup([(bsl[kk], w1_in[kk * 128:(kk + 1) * 128, 4096:4192])
                    for kk in range(8)])
            k.ops("pe", [
                (lambda e, kk=kk: e.matmul(psum[5][0:96, 0:SL], bsl[kk].ap(),
                    R_h1.ap((0, 128), (kk * SL, (kk + 1) * SL)),
                    start=(kk == 0), stop=(kk == 7)))
                for kk in range(8)],
                reads=[s.tt for s in bsl] + [R_h1.tt], writes=[ptt[5]])
            k.op("act", lambda e: e.activation(out=R_rawbc.ap(), in_=psum[5][0:64, 0:SL],
                 func=AF.Silu), reads=[ptt[5]], writes=[R_rawbc.tt])
            k.op("dve", lambda e: e.tensor_scalar(R_dtm.ap(), psum[5][64:96, 0:SL],
                 R_dtb.ap(), 0.0, ALU.add, ALU.max), reads=[ptt[5], R_dtb.tt],
                 writes=[R_dtm.tt])
            k.op("act", lambda e: e.activation(out=R_dta.ap(), in_=psum[5][64:96, 0:SL],
                 func=AF.Abs, bias=R_dtb.ap()), reads=[ptt[5], R_dtb.tt], writes=[R_dta.tt])
            k.op("act", lambda e: e.activation(out=R_dta2.ap(), in_=R_dta.ap(),
                 func=AF.Exp, scale=-1.0), reads=[R_dta.tt], writes=[R_dta2.tt])
            k.op("act", lambda e: e.activation(out=R_dta.ap(), in_=R_dta2.ap(),
                 func=AF.Ln, bias=1.0), reads=[R_dta2.tt], writes=[R_dta.tt])
            k.op("dve", lambda e: e.tensor_add(R_dt.ap(), R_dtm.ap(), R_dta.ap()),
                 reads=[R_dtm.tt, R_dta.tt], writes=[R_dt.tt])
            k.ops("pe", [lambda e: e.matmul(psum[4][0:64, 0:SL], R_swap.ap(),
                         R_rawbc.ap(), start=True, stop=True)],
                  reads=[R_swap.tt, R_rawbc.tt], writes=[ptt[4]])
            k.op("dve", lambda e: e.tensor_mul(R_ropB.ap(), R_rawbc.ap((0, 32), (0, SL)),
                 R_csA.ap((0, 32), (0, SL))), reads=[R_rawbc.tt, R_csA.tt],
                 writes=[R_ropB.tt])
            k.op("dve", lambda e: e.tensor_mul(R_ropC.ap(), R_rawbc.ap((32, 64), (0, SL)),
                 R_csA.ap((32, 64), (0, SL))), reads=[R_rawbc.tt, R_csA.tt],
                 writes=[R_ropC.tt])
            k.op("dve", lambda e: e.tensor_tensor(R_dtm.ap(), psum[4][0:32, 0:SL],
                 R_csA.ap((64, 96), (0, SL)), ALU.mult), reads=[ptt[4], R_csA.tt],
                 writes=[R_dtm.tt])
            k.op("dve", lambda e: e.tensor_add(R_ropB.ap(), R_ropB.ap(), R_dtm.ap()),
                 reads=[R_ropB.tt, R_dtm.tt], writes=[R_ropB.tt])
            k.op("dve", lambda e: e.tensor_tensor(R_dta.ap((0, 32), (0, SL)),
                 psum[4][32:64, 0:SL], R_csB.ap((0, 32), (0, SL)), ALU.mult),
                 reads=[ptt[4], R_csB.tt], writes=[R_dta.tt])
            k.op("dve", lambda e: e.tensor_add(R_ropC.ap(), R_ropC.ap(),
                 R_dta.ap((0, 32), (0, SL))), reads=[R_ropC.tt, R_dta.tt],
                 writes=[R_ropC.tt])
            k.op("dve", lambda e: e.tensor_scalar(R_a.ap(), R_dt.ap(), R_acol.ap(), None,
                 ALU.mult), reads=[R_dt.tt, R_acol.tt], writes=[R_a.tt])
            for c in range(SL // CH):
                k.op("dve", lambda e, c=c: e.tensor_tensor_scan(
                    R_acs.ap((0, 32), (c * CH, (c + 1) * CH)),
                    R_a.ap((0, 32), (c * CH, (c + 1) * CH)),
                    R_a.ap((0, 32), (c * CH, (c + 1) * CH)), 0.0, ALU.add, ALU.bypass),
                    reads=[R_a.tt], writes=[R_acs.tt])
                k.op("dve", lambda e, c=c: e.reduce_sum(
                    R_csum.ap((0, 32), (c, c + 1)),
                    R_a.ap((0, 32), (c * CH, (c + 1) * CH)),
                    axis=AX.X), reads=[R_a.tt], writes=[R_csum.tt])
                k.op("act", lambda e, c=c: e.activation(
                    out=R_dec.ap((0, 32), (c * CH, (c + 1) * CH)),
                    in_=R_acs.ap((0, 32), (c * CH, (c + 1) * CH)), func=AF.Exp,
                    scale=-1.0, bias=R_csum.ap((0, 32), (c, c + 1))),
                    reads=[R_acs.tt, R_csum.tt], writes=[R_dec.tt])
            k.op("act", lambda e: e.activation(out=R_eacs.ap(), in_=R_acs.ap(),
                 func=AF.Exp), reads=[R_acs.tt], writes=[R_eacs.tt])
            k.op("act", lambda e: e.activation(out=R_ecs.ap((0, HN), (0, NCH)),
                 in_=R_csum.ap((0, HN), (0, NCH)),
                 func=AF.Exp), reads=[R_csum.tt], writes=[R_ecs.tt])
            k.ops("pe", [
                (lambda e, h=h: e.matmul(psum[3][0:HN, h * NCH:(h + 1) * NCH],
                    R_sel64.ap((0, HN), (h * 64, h * 64 + 32)),
                    R_ecs.ap((0, HN), (0, NCH)),
                    start=True, stop=True))
                for h in range(HN)],
                reads=[R_sel64.tt, R_ecs.tt], writes=[ptt[3]])
            k.op("dve", lambda e: e.tensor_copy(R_ebc.ap((0, HN), (0, HN * NCH)),
                 psum[3][0:HN, 0:HN * NCH]),
                 reads=[ptt[3]], writes=[R_ebc.tt])
            for cp in range(NCP):
                pp = cp % 2
                cc = cp * 128
                k.ops("pe", [lambda e, cc=cc: e.transpose(psum[0][:, 0:32],
                             R_acs.ap((0, 32), (cc, cc + 128)),
                             R_id.ap((0, 32), (0, 32)))],
                      reads=[R_acs.tt, R_id.tt], writes=[ptt[0]])
                k.ops("pe", [lambda e, cc=cc: e.transpose(psum[0][:, 32:64],
                             R_dec.ap((0, 32), (cc, cc + 128)),
                             R_id.ap((0, 32), (0, 32)))],
                      reads=[R_dec.tt, R_id.tt], writes=[ptt[0]])
                k.ops("pe", [lambda e, cc=cc: e.transpose(psum[0][:, 64:96],
                             R_dt.ap((0, 32), (cc, cc + 128)),
                             R_id.ap((64, 96), (64, 96)))],
                      reads=[R_dt.tt, R_id.tt], writes=[ptt[0]])
                k.ops("pe", [lambda e, cc=cc: e.transpose(psum[0][:, 96:128],
                             R_ropB.ap((0, 32), (cc, cc + 128)),
                             R_id.ap((64, 96), (64, 96)))],
                      reads=[R_ropB.tt, R_id.tt], writes=[ptt[0]])
                k.op("dve", lambda e, pp=pp: e.tensor_scalar_mul(R_nacs[pp].ap(),
                     psum[0][:, 0:32], -1.0), reads=[ptt[0]], writes=[R_nacs[pp].tt])
                k.op("dve", lambda e, pp=pp: e.tensor_copy(R_dectk[pp].ap(),
                     psum[0][:, 32:64]), reads=[ptt[0]], writes=[R_dectk[pp].tt])
                k.op("dve", lambda e, pp=pp: e.tensor_copy(R_dttk[pp].ap(),
                     psum[0][:, 64:96]), reads=[ptt[0]], writes=[R_dttk[pp].tt])
                k.op("dve", lambda e, pp=pp: e.tensor_copy(R_bktk[pp].ap(),
                     psum[0][:, 96:128]), reads=[ptt[0]], writes=[R_bktk[pp].tt])
                k.op("dve", lambda e, pp=pp: e.tensor_mul(R_dectk[pp].ap(),
                     R_dectk[pp].ap(), R_dttk[pp].ap()),
                     reads=[R_dectk[pp].tt, R_dttk[pp].tt], writes=[R_dectk[pp].tt])
                k.op("dve", lambda e, pp=pp: e.tensor_tensor(
                    R_bka.ap().rearrange("p (h n) -> p h n", n=32),
                    bass.AP(R_bktk[pp].t, R_bktk[pp].c0,
                            [[R_bktk[pp].t.shape[1], 128], [0, HN], [1, 32]]),
                    bass.AP(R_dectk[pp].t, R_dectk[pp].c0,
                            [[R_dectk[pp].t.shape[1], 128], [1, HN], [0, 32]]),
                    ALU.mult),
                    reads=[R_bktk[pp].tt, R_dectk[pp].tt], writes=[R_bka.tt])
                k.ops("pe", [lambda e, cc=cc: e.matmul(psum[5][0:64, 0:64],
                             R_ropB.ap((0, 32), (cc, cc + 64)),
                             R_ropC.ap((0, 32), (cc, cc + 64)), start=True, stop=True,
                             skip_group_check=True)],
                      reads=[R_ropB.tt, R_ropC.tt], writes=[ptt[5]])
                k.ops("pe", [lambda e, cc=cc: e.matmul(psum[5][64:128, 0:64],
                             R_ropB.ap((0, 32), (cc + 64, cc + 128)),
                             R_ropC.ap((0, 32), (cc + 64, cc + 128)), start=True,
                             stop=True, skip_group_check=True,
                             tile_position=(64, 64))],
                      reads=[R_ropB.tt, R_ropC.tt], writes=[ptt[5]])
                k.op("dve", lambda e, pp=pp: e.tensor_tensor(R_scm[pp].ap(), R_ut.ap(),
                     psum[5][:, 0:64], ALU.mult), reads=[R_ut.tt, ptt[5]],
                     writes=[R_scm[pp].tt])
                def m_front(h):
                    rb_ = 5 + (h % 2)
                    k.ops("pe", [
                        (lambda e, h=h, cc=cc, rb_=rb_: e.matmul(
                            psum[rb_][0:64, 0:64],
                            R_sel64.ap((0, HN), (h * 64, (h + 1) * 64)),
                            R_acs.ap((0, 32), (cc, cc + 64)), start=True, stop=True,
                            skip_group_check=True)),
                        (lambda e, h=h, cc=cc, rb_=rb_: e.matmul(
                            psum[rb_][64:128, 0:64],
                            R_sel64.ap((0, HN), (h * 64, (h + 1) * 64)),
                            R_acs.ap((0, 32), (cc + 64, cc + 128)), start=True,
                            stop=True, skip_group_check=True, tile_position=(0, 64)))],
                        reads=[R_sel64.tt, R_acs.tt], writes=[ptt[rb_]])
                    k.op("dve", lambda e, pp=pp, h=h, rb_=rb_: e.tensor_scalar(
                        R_mdf[h % 2].ap(), psum[rb_][:, 0:64],
                        R_nacs[pp].ap((0, 128), (h, h + 1)), 0.0, ALU.add, ALU.min),
                        reads=[ptt[rb_], R_nacs[pp].tt],
                        writes=[R_mdf[h % 2].tt])
                    k.op("act", lambda e, h=h: e.activation(
                        out=R_mex[h % 4].ap(),
                        in_=R_mdf[h % 2].ap(), func=AF.Exp),
                        reads=[R_mdf[h % 2].tt], writes=[R_mex[h % 4].tt])

                def m_back(h):
                    k.op("dve", lambda e, pp=pp, h=h: e.scalar_tensor_tensor(
                        R_mta.ap((0, 128), (h * 64, (h + 1) * 64)),
                        R_mex[h % 4].ap(),
                        R_dttk[pp].ap((0, 128), (h, h + 1)), R_scm[pp].ap(),
                        ALU.mult, ALU.mult),
                        reads=[R_mex[h % 4].tt, R_dttk[pp].tt, R_scm[pp].tt],
                        writes=[R_mta.tt])
                m_front(0)
                m_front(1)
                for h in range(HN):
                    if h + 2 < HN:
                        m_front(h + 2)
                    m_back(h)
                for hp in range(HN // 2):
                    for hh in range(2):
                        h = hp * 2 + hh
                        ob = hh * 64
                        xsl0 = lambda h=h, cp=cp: R_xst.ap(
                            (0, 64), (cp * DI_ + h * 64, cp * DI_ + (h + 1) * 64))
                        xsl1 = lambda h=h, cp=cp: R_xst.ap(
                            (64, 128), (cp * DI_ + h * 64, cp * DI_ + (h + 1) * 64))
                        k.ops("pe", [
                            (lambda e, h=h, ob=ob, xsl0=xsl0: e.matmul(
                                psum[1][ob:ob + 64, 0:64], xsl0(),
                                R_mta.ap((0, 64), (h * 64, (h + 1) * 64)),
                                start=True, stop=False, skip_group_check=True,
                                tile_position=(0, ob)))],
                            reads=[R_xst.tt, R_mta.tt], writes=[ptt[1]])
                        k.ops("pe", [
                            (lambda e, h=h, ob=ob, xsl1=xsl1: e.matmul(
                                psum[1][ob:ob + 64, 64:128], xsl1(),
                                R_mta.ap((64, 128), (h * 64, (h + 1) * 64)),
                                start=False, stop=False, skip_group_check=True,
                                tile_position=(64, ob)))],
                            reads=[R_xst.tt, R_mta.tt], writes=[ptt[1]])
                        k.ops("pe", [
                            (lambda e, h=h, ob=ob, xsl0=xsl0: e.matmul(
                                psum[1][ob:ob + 64, 0:64], xsl0(),
                                R_dident.ap((0, 64), (h * 64, (h + 1) * 64)),
                                start=False, stop=False, skip_group_check=True,
                                tile_position=(0, ob)))],
                            reads=[R_xst.tt, R_dident.tt], writes=[ptt[1]])
                        k.ops("pe", [
                            (lambda e, h=h, ob=ob, xsl1=xsl1: e.matmul(
                                psum[1][ob:ob + 64, 64:128], xsl1(),
                                R_dident.ap((64, 128), (h * 64, (h + 1) * 64)),
                                start=False, stop=False, skip_group_check=True,
                                tile_position=(64, ob)))],
                            reads=[R_xst.tt, R_dident.tt], writes=[ptt[1]])
                        k.ops("pe", [
                            (lambda e, h=h, xsl0=xsl0: e.matmul(
                                psum[2][0:32, 0:64],
                                R_bka.ap((0, 64), (h * 32, (h + 1) * 32)),
                                xsl0(), start=True, stop=True, skip_group_check=True))],
                            reads=[R_bka.tt, R_xst.tt], writes=[ptt[2]])
                        k.ops("pe", [
                            (lambda e, h=h, xsl1=xsl1: e.matmul(
                                psum[2][32:64, 0:64],
                                R_bka.ap((64, 128), (h * 32, (h + 1) * 32)),
                                xsl1(), start=True, stop=True, skip_group_check=True,
                                tile_position=(64, 32)))],
                            reads=[R_bka.tt, R_xst.tt], writes=[ptt[2]])
                        k.ops("pe", [
                            (lambda e, h=h, cc=cc: e.matmul(psum[3][0:HN, 0:128],
                                R_sel64.ap((0, HN), (h * 64, h * 64 + 32)),
                                R_eacs.ap((0, 32), (cc, cc + 128)), start=True,
                                stop=True, skip_group_check=True))],
                            reads=[R_sel64.tt, R_eacs.tt], writes=[ptt[3]])
                        k.op("dve", lambda e, cc=cc, h=h: e.tensor_tensor(
                            R_cks[h % 2].ap(), R_ropC.ap((0, 32), (cc, cc + 128)),
                            psum[3][0:HN, 0:128], ALU.mult),
                            reads=[R_ropC.tt, ptt[3]], writes=[R_cks[h % 2].tt])
                        for ci in range(2):
                            csl = cp * 2 + ci
                            k.ops("pe", [
                                (lambda e, h=h, ci=ci, ob=ob, hh=hh: e.matmul(
                                    psum[1][ob:ob + 64, ci * 64:(ci + 1) * 64],
                                    R_sal[h].ap(),
                                    R_cks[h % 2].ap((0, HN), (ci * 64, (ci + 1) * 64)),
                                    start=False, stop=(hh == 1 and ci == 1),
                                    skip_group_check=True, tile_position=(32, ob)))],
                                reads=[R_sal[h].tt, R_cks[h % 2].tt], writes=[ptt[1]])
                            k.op("dve", lambda e, h=h, ci=ci, csl=csl:
                                 e.scalar_tensor_tensor(
                                    R_sal[h].ap(), R_sal[h].ap(),
                                    R_ebc.ap((0, HN), (h * NCH + csl, h * NCH + csl + 1)),
                                    psum[2][ci * 32:(ci + 1) * 32, 0:64],
                                    ALU.mult, ALU.add),
                                 reads=[R_sal[h].tt, R_ebc.tt, ptt[2]],
                                 writes=[R_sal[h].tt])
                    k.op("dve", lambda e, hp=hp, cp=cp: e.tensor_copy(
                        R_y.ap((0, 128), (hp * SL + cp * 128, hp * SL + (cp + 1) * 128)),
                        psum[1][:, 0:128]), reads=[ptt[1]], writes=[R_y.tt])
            if dev:
                ch_ydbg = k.chan("ch_ydbg")
                for fc in range(16):
                    k.dma("pool", ch_ydbg, y_dbg[b, fc * 128:(fc + 1) * 128, u0:u0 + SL],
                          R_y.ap((0, 128), (fc * SL, (fc + 1) * SL)), reads=[R_y.tt])
            R_sq = SM.carve("lnsq", 128, 512, c0=0)
            rstd = SM.carve("ln_rstd", 1, 512, c0=1024)
            msq = SM.carve("ln_msq", 1, 512, c0=1536)
            var = SM.carve("ln_var", 1, 512, c0=2048)
            for fc in range(16):
                k.op("dve", lambda e, fc=fc: e.tensor_mul(
                    R_yg.ap((0, 128), (fc * SL, (fc + 1) * SL)),
                    R_y.ap((0, 128), (fc * SL, (fc + 1) * SL)),
                    R_sz.ap((0, 128), (fc * SL, (fc + 1) * SL))),
                    reads=[R_y.tt, R_sz.tt], writes=[R_yg.tt])
                k.op("act", lambda e, fc=fc: e.activation(out=R_sq.ap((0, 128), (0, SL)),
                     in_=R_yg.ap((0, 128), (fc * SL, (fc + 1) * SL)), func=AF.Square),
                     reads=[R_yg.tt], writes=[R_sq.tt])
                k.ops("pe", [lambda e, fc=fc: e.matmul(psum[6][0:1, 0:SL],
                             R_onec.ap(), R_sq.ap((0, 128), (0, SL)),
                             start=(fc == 0), stop=(fc == 15))],
                      reads=[R_onec.tt, R_sq.tt], writes=[ptt[6]])
            k.op("dve", lambda e: e.tensor_scalar_mul(var.ap((0, 1), (0, SL)),
                 psum[6][0:1, 0:SL], 1.0 / DI_), reads=[ptt[6]], writes=[var.tt])
            k.op("dve", lambda e: e.tensor_scalar_add(var.ap((0, 1), (0, SL)),
                 var.ap((0, 1), (0, SL)), 1e-6), reads=[var.tt], writes=[var.tt])
            k.op("act", lambda e: e.activation(out=msq.ap((0, 1), (0, SL)),
                 in_=var.ap((0, 1), (0, SL)), func=AF.Ln),
                 reads=[var.tt], writes=[msq.tt])
            k.op("act", lambda e: e.activation(out=rstd.ap((0, 1), (0, SL)),
                 in_=msq.ap((0, 1), (0, SL)), func=AF.Exp, scale=-0.5),
                 reads=[msq.tt], writes=[rstd.tt])
            k.ops("pe", [lambda e: e.matmul(psum[7][0:128, 0:SL], R_ones.ap(),
                         rstd.ap((0, 1), (0, SL)), start=True, stop=True)],
                  reads=[R_ones.tt, rstd.tt], writes=[ptt[7]])
            for fc in range(16):
                k.op("dve", lambda e, fc=fc: e.tensor_mul(
                    R_yg.ap((0, 128), (fc * SL, (fc + 1) * SL)),
                    R_yg.ap((0, 128), (fc * SL, (fc + 1) * SL)), psum[7][0:128, 0:SL]),
                    reads=[R_yg.tt, ptt[7]], writes=[R_yg.tt])
                k.op("dve", lambda e, fc=fc: e.tensor_scalar(
                    R_y.ap((0, 128), (fc * SL, (fc + 1) * SL)),
                    R_yg.ap((0, 128), (fc * SL, (fc + 1) * SL)),
                    R_rmsw.ap((0, 128), (fc, fc + 1)), None, ALU.mult),
                    reads=[R_yg.tt, R_rmsw.tt], writes=[R_y.tt])
            for dgrp in range(2):
                for kh in range(2):
                    osl = wslots(512, [(w1_out[kh * 1024:(kh + 1) * 1024,
                                               dgrp * 512:(dgrp + 1) * 512], 8)])
                    for mo in range(4):
                        k.ops("pe", [
                            (lambda e, kk=kk, kh=kh, mo=mo, osl=osl: e.matmul(
                                psum[mo][:, 0:SL],
                                osl[kk].ap((0, 128), (mo * 128, (mo + 1) * 128)),
                                R_y.ap((0, 128), ((kh * 8 + kk) * SL, (kh * 8 + kk + 1) * SL)),
                                start=(kh == 0 and kk == 0), stop=(kh == 1 and kk == 7)))
                            for kk in range(8)],
                            reads=[s.tt for s in osl] + [R_y.tt], writes=[ptt[mo]])
                for mo in range(4):
                    dmo = dgrp * 4 + mo
                    k.op("dve", lambda e, mo=mo, dmo=dmo, u0=u0: e.tensor_add(
                        R_yg.ap((0, 128), (mo * SL, (mo + 1) * SL)), psum[mo][:, 0:SL],
                        xnap(dmo, u0, u0 + SL)),
                        reads=[ptt[mo], R_xn.tt], writes=[R_yg.tt])
                    k.dma("sp", ch_s1, s1_d[b, dmo * 128:(dmo + 1) * 128, u0:u0 + SL],
                          R_yg.ap((0, 128), (mo * SL, (mo + 1) * SL)), reads=[R_yg.tt])
        tt_s1 = TT("s1_dram")
        tt_s1.w["ch_s1"] = ch_s1.cnt
        return tt_s1

    def moe_fin(b, tt_x1, tt_s1, ch_out):
        R_xn = XN.live["xn"]
        xnap = lambda c, t0, t1: R_xn.ap((0, 128), (c * T + t0, c * T + t1))
        J5 = min(512, TM)
        for tm in range(T // TM):
            u0 = tm * TM
            for A in (AB, CB, WB, FA, SM):
                A.new_phase()
            R_ptok = SM.carve("ptok", 128, 8 * (TM // 128), c0=2560)
            R_pfm = SM.carve("pfm", NE, TM, r0=64, c0=1536)
            R_lgt = SM.carve("lgt", NE, 1024, c0=2688)
            R_m1 = [SM.carve(f"m1_{pp}", 128, 1, c0=3712 + pp) for pp in range(2)]
            R_m2 = [SM.carve(f"m2_{pp}", 128, 1, c0=3714 + pp) for pp in range(2)]
            R_msk = [SM.carve(f"msk{pp}", 128, 8, c0=3720 + pp * 8) for pp in range(2)]
            R_exp = [SM.carve(f"mexp{pp}", 128, 8, c0=3736 + pp * 8) for pp in range(2)]
            R_sum = [SM.carve(f"msum{pp}", 128, 1, c0=3752 + pp) for pp in range(2)]
            R_e2 = [SM.carve(f"me2_{pp}", 128, 1, c0=3754 + pp) for pp in range(2)]
            R_den = [SM.carve(f"mden{pp}", 128, 1, c0=3756 + pp) for pp in range(2)]
            R_he = AB.carve("he", 128, 16 * TM, c0=0)
            R_g1 = CB.carve("g1fm", 128, 8 * TM, c0=0)
            R_gacc = FA.carve("gacc", 128, 8 * TM, c0=0)
            for j in range(TM // J5):
                t0 = u0 + j * J5
                k.ops("pe", [
                    (lambda e, kk=kk, t0=t0: e.matmul(psum[0][0:NE, 0:J5],
                        R_rtrb.ap((0, 128), (kk * 8, (kk + 1) * 8)),
                        xnap(kk, t0, t0 + J5), start=(kk == 0), stop=(kk == 7)))
                    for kk in range(8)],
                    reads=[R_rtrb.tt, R_xn.tt], writes=[ptt[0]])
                jo = (j % 2) * J5
                k.op("dve", lambda e, jo=jo: e.tensor_copy(
                     R_lgt.ap((0, NE), (jo, jo + J5)), psum[0][0:NE, 0:J5]),
                     reads=[ptt[0]], writes=[R_lgt.tt])
                def emit_top2(tc4, tcg, jo):
                    pp = tcg % 2
                    tb_ = 6 + pp
                    m1, m2, msk = R_m1[pp], R_m2[pp], R_msk[pp]
                    mex, msum, me2, mden = R_exp[pp], R_sum[pp], R_e2[pp], R_den[pp]
                    k.ops("pe", [lambda e, tc4=tc4, jo=jo, tb_=tb_: e.transpose(
                                 psum[tb_][:, 0:NE],
                                 R_lgt.ap((0, NE), (jo + tc4 * 128, jo + (tc4 + 1) * 128)),
                                 R_id.ap((0, NE), (0, NE)))],
                          reads=[R_lgt.tt, R_id.tt], writes=[ptt[tb_]])
                    lg = lambda tb_=tb_: psum[tb_][:, 0:NE]
                    k.op("dve", lambda e: e.reduce_max(m1.ap(), lg(), axis=AX.X),
                         reads=[ptt[tb_]], writes=[m1.tt])
                    k.op("dve", lambda e: e.tensor_scalar(msk.ap(), lg(),
                         m1.ap(), None, ALU.is_equal),
                         reads=[ptt[tb_], m1.tt], writes=[msk.tt])
                    k.op("dve", lambda e: e.scalar_tensor_tensor(msk.ap(), msk.ap(),
                         -1e30, lg(), ALU.mult, ALU.add),
                         reads=[msk.tt, ptt[tb_]], writes=[msk.tt])
                    k.op("dve", lambda e: e.reduce_max(m2.ap(), msk.ap(), axis=AX.X),
                         reads=[msk.tt], writes=[m2.tt])
                    k.op("dve", lambda e: e.tensor_scalar_mul(me2.ap(), m1.ap(), -1.0),
                         reads=[m1.tt], writes=[me2.tt])
                    k.op("act", lambda e: e.activation(out=mex.ap(), in_=lg(),
                         func=AF.Exp, bias=me2.ap()),
                         reads=[ptt[tb_], me2.tt], writes=[mex.tt])
                    k.op("dve", lambda e: e.reduce_sum(msum.ap(), mex.ap(), axis=AX.X),
                         reads=[mex.tt], writes=[msum.tt])
                    k.op("dve", lambda e: e.tensor_sub(mden.ap(), m2.ap(), m1.ap()),
                         reads=[m2.tt, m1.tt], writes=[mden.tt])
                    k.op("act", lambda e: e.activation(out=me2.ap(), in_=mden.ap(),
                         func=AF.Exp), reads=[mden.tt], writes=[me2.tt])
                    k.op("dve", lambda e: e.scalar_tensor_tensor(mden.ap(), msum.ap(),
                         1e-9, me2.ap(), ALU.mult, ALU.add),
                         reads=[msum.tt, me2.tt], writes=[mden.tt])
                    k.op("dve", lambda e: e.tensor_scalar_add(mden.ap(), mden.ap(), 1.0),
                         reads=[mden.tt], writes=[mden.tt])
                    k.op("dve", lambda e: e.reciprocal(mden.ap(), mden.ap()),
                         reads=[mden.tt], writes=[mden.tt])
                    k.op("dve", lambda e: e.tensor_scalar(msk.ap(), lg(),
                         m2.ap(), None, ALU.is_ge),
                         reads=[ptt[tb_], m2.tt], writes=[msk.tt])
                    k.op("dve", lambda e: e.tensor_mul(mex.ap(), mex.ap(), msk.ap()),
                         reads=[mex.tt, msk.tt], writes=[mex.tt])
                    k.op("dve", lambda e, tcg=tcg: e.tensor_scalar(
                         R_ptok.ap((0, 128), (tcg * 8, (tcg + 1) * 8)), mex.ap(),
                         mden.ap(), None, ALU.mult),
                         reads=[mex.tt, mden.tt], writes=[R_ptok.tt])
                for tc4 in range(J5 // 128):
                    emit_top2(tc4, j * (J5 // 128) + tc4, jo)
            for tc4 in range(TM // J5):
                nt4 = J5 // 128
                k.ops("pe", [
                    (lambda e, tc=tc, tc4=tc4, nt4=nt4: e.transpose(
                        psum[2][0:NE, tc * 128:(tc + 1) * 128],
                        R_ptok.ap((0, 128), ((tc4 * nt4 + tc) * 8,
                                             (tc4 * nt4 + tc + 1) * 8)),
                        R_id.ap()))
                    for tc in range(nt4)],
                    reads=[R_ptok.tt, R_id.tt], writes=[ptt[2]])
                k.op("dve", lambda e, tc4=tc4: e.tensor_copy(
                    R_pfm.ap((0, NE), (tc4 * J5, (tc4 + 1) * J5)),
                    psum[2][0:NE, 0:J5]),
                    reads=[ptt[2]], writes=[R_pfm.tt])
            for ei in range(NE):
                for fg in range(4):
                    gsl = wslots(512, [(mgw[ei, :, fg * 512:(fg + 1) * 512], 8)])
                    for j in range(TM // J5):
                        for mi in range(4):
                            m = fg * 4 + mi
                            bank = (fg * 4 + mi) % 6
                            k.ops("pe", [
                                (lambda e, kk=kk, mi=mi, j=j, bank=bank, gsl=gsl, u0=u0: e.matmul(
                                    psum[bank][:, 0:J5],
                                    gsl[kk].ap((0, 128), (mi * 128, (mi + 1) * 128)),
                                    xnap(kk, u0 + j * J5, u0 + (j + 1) * J5),
                                    start=(kk == 0), stop=(kk == 7)))
                                for kk in range(8)],
                                reads=[s.tt for s in gsl] + [R_xn.tt], writes=[ptt[bank]])
                            k.op("act", lambda e, m=m, j=j, bank=bank, ei=ei: e.activation(
                                out=R_he.ap((0, 128), (m * TM + j * J5,
                                                       m * TM + (j + 1) * J5)),
                                in_=psum[bank][:, 0:J5], func=AF.Silu,
                                bias=R_mgb.ap((0, 128), (ei * 16 + m, ei * 16 + m + 1))),
                                reads=[ptt[bank], R_mgb.tt], writes=[R_he.tt])
                for fg in range(4):
                    usl = wslots(512, [(muw[ei, :, fg * 512:(fg + 1) * 512], 8)])
                    for j in range(TM // J5):
                        for mi in range(4):
                            m = fg * 4 + mi
                            bank = (fg * 4 + mi) % 6
                            k.ops("pe", [
                                (lambda e, kk=kk, mi=mi, j=j, bank=bank, usl=usl, u0=u0: e.matmul(
                                    psum[bank][:, 0:J5],
                                    usl[kk].ap((0, 128), (mi * 128, (mi + 1) * 128)),
                                    xnap(kk, u0 + j * J5, u0 + (j + 1) * J5),
                                    start=(kk == 0), stop=(kk == 7)))
                                for kk in range(8)],
                                reads=[s.tt for s in usl] + [R_xn.tt], writes=[ptt[bank]])
                            k.op("dve", lambda e, m=m, j=j, bank=bank, ei=ei:
                                 e.scalar_tensor_tensor(
                                    R_he.ap((0, 128), (m * TM + j * J5,
                                                       m * TM + (j + 1) * J5)),
                                    psum[bank][:, 0:J5],
                                    R_mub.ap((0, 128), (ei * 16 + m, ei * 16 + m + 1)),
                                    R_he.ap((0, 128), (m * TM + j * J5,
                                                       m * TM + (j + 1) * J5)),
                                    ALU.add, ALU.mult),
                                 reads=[ptt[bank], R_mub.tt, R_he.tt], writes=[R_he.tt])
                for dj in range(2):
                    for kh in range(2):
                        dsl = wslots(512, [(mdw[ei, kh * 1024:(kh + 1) * 1024,
                                                dj * 512:(dj + 1) * 512], 8)])
                        for tc in range(TM // 128):
                            bank = tc % 8
                            k.ops("pe", [
                                (lambda e, kk=kk, kh=kh, tc=tc, bank=bank, dsl=dsl: e.matmul(
                                    psum[bank][:, 0:512],
                                    R_he.ap((0, 128), ((kh * 8 + kk) * TM + tc * 128,
                                                       (kh * 8 + kk) * TM + (tc + 1) * 128)),
                                    dsl[kk].ap(), start=(kh == 0 and kk == 0),
                                    stop=(kh == 1 and kk == 7)))
                                for kk in range(8)],
                                reads=[s.tt for s in dsl] + [R_he.tt], writes=[ptt[bank]])
                    for tc in range(TM // 128):
                        bank = tc % 8
                        gap = lambda tc=tc, dj=dj: R_gacc.ap(
                            (0, 128), (tc * 1024 + dj * 512, tc * 1024 + (dj + 1) * 512))
                        pcol = lambda tc=tc, ei=ei: R_ptok.ap((0, 128),
                            (tc * 8 + ei, tc * 8 + ei + 1))
                        if ei == 0:
                            k.op("dve", lambda e, bank=bank, gap=gap, pcol=pcol:
                                 e.tensor_scalar(gap(), psum[bank][:, 0:512], pcol(),
                                                 None, ALU.mult),
                                 reads=[ptt[bank], R_ptok.tt], writes=[R_gacc.tt])
                        else:
                            k.op("dve", lambda e, bank=bank, gap=gap, pcol=pcol:
                                 e.scalar_tensor_tensor(gap(), psum[bank][:, 0:512],
                                                        pcol(), gap(), ALU.mult, ALU.add),
                                 reads=[ptt[bank], R_ptok.tt, R_gacc.tt],
                                 writes=[R_gacc.tt])
            for tc in range(TM // 128):
                for dj in range(2):
                    bank = (tc * 2 + dj) % 8
                    k.ops("pe", [
                        (lambda e, tc=tc, dj=dj, bank=bank: e.matmul(
                            psum[bank][:, 0:512],
                            R_pfm.ap((0, NE), (tc * 128, (tc + 1) * 128)),
                            R_mdbblk.ap((0, NE), (dj * 512, (dj + 1) * 512)),
                            start=True, stop=True))],
                        reads=[R_pfm.tt, R_mdbblk.tt], writes=[ptt[bank]])
                    k.op("dve", lambda e, tc=tc, dj=dj, bank=bank: e.tensor_add(
                        R_gacc.ap((0, 128), (tc * 1024 + dj * 512,
                                             tc * 1024 + (dj + 1) * 512)),
                        R_gacc.ap((0, 128), (tc * 1024 + dj * 512,
                                             tc * 1024 + (dj + 1) * 512)),
                        psum[bank][:, 0:512]),
                        reads=[R_gacc.tt, ptt[bank]], writes=[R_gacc.tt])
            for tc in range(TM // 128):
                for c in range(8):
                    bank = c % 2
                    k.ops("pe", [lambda e, tc=tc, c=c, bank=bank: e.transpose(
                        psum[bank][:, 0:128],
                        R_gacc.ap((0, 128), (tc * 1024 + c * 128,
                                             tc * 1024 + (c + 1) * 128)),
                        R_id.ap())],
                        reads=[R_gacc.tt, R_id.tt], writes=[ptt[bank]])
                    k.op("act", lambda e, tc=tc, c=c, bank=bank: e.activation(
                        out=R_g1.ap((0, 128), (c * TM + tc * 128, c * TM + (tc + 1) * 128)),
                        in_=psum[bank][:, 0:128], func=AF.Silu),
                        reads=[ptt[bank]], writes=[R_g1.tt])
            if dev:
                ch_g1d = k.chan(f"chg{b}_{tm}")
                for tc in range(TM // 128):
                    k.dma("pool", ch_g1d,
                          ga_dbg[b, u0 + tc * 128:u0 + (tc + 1) * 128, :],
                          R_gacc.ap((0, 128), (tc * 1024, (tc + 1) * 1024)),
                          reads=[R_gacc.tt])
                for c in range(8):
                    k.dma("pool", ch_g1d, g1_dbg[b, c * 128:(c + 1) * 128, u0:u0 + TM],
                          R_g1.ap((0, 128), (c * TM, (c + 1) * TM)), reads=[R_g1.tt])
                for tc in range(TM // 128):
                    k.dma("pool", ch_g1d,
                          p_dbg[b, u0 + tc * 128:u0 + (tc + 1) * 128, :],
                          R_ptok.ap((0, 128), (tc * 8, (tc + 1) * 8)),
                          reads=[R_ptok.tt])
                for _tt in (R_gacc.tt, R_g1.tt, R_ptok.tt):
                    _tt.readers[ch_g1d.name] = ch_g1d.cnt
            FA.new_phase()
            R_x1 = FA.carve("fx1", 128, 8 * J5, c0=0)
            R_s1 = FA.carve("fs1", 128, 8 * J5, c0=8 * J5)
            ch_f = k.chan("ch_f")
            for j in range(TM // J5):
                t0 = u0 + j * J5
                k.dma("sp", ch_f, R_x1.ap().rearrange("p (c t) -> p c t", c=8),
                      x1_d[b, :, t0:t0 + J5].rearrange("(c p) t -> p c t", p=128),
                      reads=[tt_x1], writes=[R_x1.tt])
                k.dma("sp", k.chan("ch_fs"), R_s1.ap().rearrange("p (c t) -> p c t", c=8),
                      s1_d[b, :, t0:t0 + J5].rearrange("(c p) t -> p c t", p=128),
                      reads=[tt_s1], writes=[R_s1.tt])
                for c in range(8):
                    k.op("dve", lambda e, c=c, j=j: e.tensor_mul(
                        R_s1.ap((0, 128), (c * J5, (c + 1) * J5)),
                        R_s1.ap((0, 128), (c * J5, (c + 1) * J5)),
                        R_g1.ap((0, 128), (c * TM + j * J5, c * TM + (j + 1) * J5))),
                        reads=[R_s1.tt, R_g1.tt], writes=[R_s1.tt])
                    k.op("dve", lambda e, c=c: e.tensor_add(
                        R_x1.ap((0, 128), (c * J5, (c + 1) * J5)),
                        R_x1.ap((0, 128), (c * J5, (c + 1) * J5)),
                        R_s1.ap((0, 128), (c * J5, (c + 1) * J5))),
                        reads=[R_x1.tt, R_s1.tt], writes=[R_x1.tt])
                for tb in range(J5 // 128):
                    stage = FA.alias(R_s1, 0, 128, tb * 1024, 1024)
                    for c in range(8):
                        bank = c % 2
                        k.ops("pe", [lambda e, tb=tb, c=c, bank=bank: e.transpose(
                            psum[bank][:, 0:128],
                            R_x1.ap((0, 128), (c * J5 + tb * 128,
                                               c * J5 + (tb + 1) * 128)),
                            R_id.ap())],
                            reads=[R_x1.tt, R_id.tt], writes=[ptt[bank]])
                        k.op("dve", lambda e, c=c, bank=bank, stage=stage: e.tensor_copy(
                            stage.ap((0, 128), (c * 128, (c + 1) * 128)),
                            psum[bank][:, 0:128]),
                            reads=[ptt[bank]], writes=[stage.tt])
                    k.dma("sp", ch_out, out[b, t0 + tb * 128:t0 + (tb + 1) * 128, :],
                          stage.ap(), reads=[stage.tt])

    ch_out = k.chan("ch_out")
    for b in range(NB):
        tt_x1 = l0(b)
        tt_s1 = None
        if stages >= 2:
            tt_s1 = l1a(b)
        if stages >= 3:
            moe_fin(b, tt_x1, tt_s1, ch_out)
    for chn, chan in k.chans.items():
        t = TT("fin_" + chn)
        t.w[chn] = chan.cnt
        k.barrier_wait("pool", [t])
    k.op("pool", lambda e: e.engine_nop())

    with nc.Block() as block:
        k.build(block)
    stack.close()
    return nc, k



def kernel(**inputs):
    sh, per_core = _prep_inputs(inputs)
    nc, _ = build_core(dev=False, stages=3, Tdev=2048)
    in_maps = []
    for c in range(NCORES):
        m = dict(sh)
        m["x"] = per_core[c]
        in_maps.append(m)
    res = run_bass_kernel_spmd(nc, in_maps, core_ids=list(range(NCORES)))
    outs = [res.results[c]["out"] for c in range(NCORES)]
    return np.ascontiguousarray(np.concatenate(outs, axis=0))


# version 09
# speedup vs baseline: 1.1716x; 1.1716x over previous
import numpy as np
import ml_dtypes
from contextlib import ExitStack
import concourse.bass as bass
import concourse.mybir as mybir
from concourse.bass_utils import run_bass_kernel_spmd

F32 = mybir.dt.float32
BF16 = mybir.dt.bfloat16
I32 = mybir.dt.int32
AF = mybir.ActivationFunctionType
ALU = mybir.AluOpType
AX = mybir.AxisListType

ENG_BLOCK = {"pe": "tensor", "dve": "vector", "act": "scalar", "pool": "gpsimd", "sp": "sync"}
COMPUTE_ENGS = ("pe", "dve", "act", "pool")

D = 1024
T = 2048
NBT = 32
NCORES = 8
NB = NBT // NCORES
DI = 2048
HN = 32
HP = 64
DS = 32
CH = 64
F0 = 3072
DFF = 2048
NE = 8
S127 = 1.0 / 127.0


class TT:
    __slots__ = ("name", "w", "readers")

    def __init__(self, name):
        self.name = name
        self.w = {}
        self.readers = {}

    def merge_from(self, other):
        for s, c in other.w.items():
            if self.w.get(s, 0) < c:
                self.w[s] = c
        for s, c in other.readers.items():
            if self.readers.get(s, 0) < c:
                self.readers[s] = c


class Chan:
    __slots__ = ("name", "cnt")

    def __init__(self, name):
        self.name = name
        self.cnt = 0


class K:
    def __init__(self, nc, stack):
        self.nc = nc
        self.stack = stack
        self.streams = {e: [] for e in ENG_BLOCK}
        self.vc = {e: {} for e in ENG_BLOCK}
        self.sems = {}
        self.cnt = {}
        for e in COMPUTE_ENGS:
            self._mk_sem("S" + e)
            self.cnt["S" + e] = 0
        self.chans = {}
        self.n_inst = 0

    def _mk_sem(self, name):
        self.sems[name] = self.stack.enter_context(self.nc.semaphore(name))

    def chan(self, name):
        if name not in self.chans:
            self._mk_sem(name)
            self.chans[name] = Chan(name)
        return self.chans[name]

    def _waits(self, eng, reads, writes, prune=True):
        need = {}
        for t in reads:
            for s, c in t.w.items():
                if need.get(s, 0) < c:
                    need[s] = c
        for t in writes:
            for s, c in t.w.items():
                if need.get(s, 0) < c:
                    need[s] = c
            for s, c in t.readers.items():
                if need.get(s, 0) < c:
                    need[s] = c
        vc = self.vc[eng]
        for s in sorted(need):
            c = need[s]
            if vc.get(s, 0) < c or (not prune and need[s] > 0):
                if vc.get(s, 0) < c:
                    vc[s] = c
                h = self.sems[s]
                self.streams[eng].append(lambda e, h=h, c=c: e.wait_ge(h, c))
                self.n_inst += 1

    def op(self, eng, emit, reads=(), writes=()):
        self._waits(eng, reads, writes)
        sname = "S" + eng
        self.cnt[sname] += 1
        cafter = self.cnt[sname]
        h = self.sems[sname]
        def run(e, emit=emit, h=h):
            emit(e).then_inc(h, 1)
        self.streams[eng].append(run)
        self.n_inst += 1
        for t in writes:
            if t.w.get(sname, 0) < cafter:
                t.w[sname] = cafter
        for t in reads:
            if t.readers.get(sname, 0) < cafter:
                t.readers[sname] = cafter

    def ops(self, eng, emits, reads=(), writes=()):
        self._waits(eng, reads, writes)
        sname = "S" + eng
        self.cnt[sname] += 1
        cafter = self.cnt[sname]
        h = self.sems[sname]
        for em in emits[:-1]:
            self.streams[eng].append(lambda e, em=em: em(e))
        def run(e, em=emits[-1], h=h):
            em(e).then_inc(h, 1)
        self.streams[eng].append(run)
        self.n_inst += len(emits)
        for t in writes:
            if t.w.get(sname, 0) < cafter:
                t.w[sname] = cafter
        for t in reads:
            if t.readers.get(sname, 0) < cafter:
                t.readers[sname] = cafter

    def dma(self, queue, chan, out_ap, in_ap, reads=(), writes=(), **kw):
        self._waits(queue, reads, writes, prune=False)
        chan.cnt += 16
        cafter = chan.cnt
        h = self.sems[chan.name]
        def run(e, out_ap=out_ap, in_ap=in_ap, h=h, kw=kw):
            e.dma_start(out=out_ap, in_=in_ap, **kw).then_inc(h, 16)
        self.streams[queue].append(run)
        self.n_inst += 1
        for t in writes:
            if t.w.get(chan.name, 0) < cafter:
                t.w[chan.name] = cafter
        for t in reads:
            if t.readers.get(chan.name, 0) < cafter:
                t.readers[chan.name] = cafter

    def barrier_wait(self, eng, tts):
        self._waits(eng, tts, ())

    def build(self, block):
        for eng, attr in ENG_BLOCK.items():
            stream = self.streams[eng]
            if not stream:
                continue
            getattr(block, attr)(lambda e, stream=stream: [f(e) for f in stream])


class Reg:
    __slots__ = ("t", "r0", "nr", "c0", "ncols", "tt")

    def __init__(self, t, r0, nr, c0, ncols, tt):
        self.t, self.r0, self.nr, self.c0, self.ncols = t, r0, nr, c0, ncols
        self.tt = tt

    def ap(self, rs=None, cs=None):
        r0, r1 = (0, self.nr) if rs is None else rs
        c0, c1 = (0, self.ncols) if cs is None else cs
        assert 0 <= r0 < r1 <= self.nr and 0 <= c0 < c1 <= self.ncols, \
            (rs, cs, self.nr, self.ncols)
        return self.t[self.r0 + r0:self.r0 + r1, self.c0 + c0:self.c0 + c1]


class Arena:
    def __init__(self, K_, name, cols, dtype):
        self.K = K_
        self.name = name
        self.t = K_.stack.enter_context(K_.nc.sbuf_tensor(name, [128, cols], dtype))
        self.cols = cols
        self.live = {}
        self.legacy = TT(name + "_legacy")

    def carve(self, name, rows, cols, r0=0, c0=None):
        assert c0 is not None
        if name in self.live:
            r = self.live[name]
            assert (r.r0, r.nr, r.c0, r.ncols) == (r0, rows, c0, cols), name
            return r
        assert c0 + cols <= self.cols, (self.name, name, c0, cols, self.cols)
        tt = TT(self.name + "." + name)
        tt.merge_from(self.legacy)
        r = Reg(self.t, r0, rows, c0, cols, tt)
        self.live[name] = r
        return r

    def alias(self, reg, r0, rows, c0, cols):
        return Reg(reg.t, r0, rows, reg.c0 + c0, cols, reg.tt)

    def new_phase(self):
        for r in self.live.values():
            self.legacy.merge_from(r.tt)
        self.live = {}



def _prep_inputs(inputs):
    f = np.ascontiguousarray
    sh = {}
    sh["w0_in"] = f(inputs["L0_attn_W_in"].T)
    sh["w0_a"] = f(inputs["L0_attn_W_alpha"].T)
    sh["w0_b"] = f(inputs["L0_attn_W_beta"].T)
    sh["w0_v"] = f(inputs["L0_attn_W_v"].T)
    sh["w0_o"] = f(inputs["L0_attn_W_out"].T)
    sh["b0_a"] = f(inputs["L0_attn_b_alpha"].reshape(8, 128).T)
    sh["b0_b"] = f(inputs["L0_attn_b_beta"].reshape(8, 128).T)
    sh["b0_v"] = f(inputs["L0_attn_b_v"].reshape(8, 128).T)
    sh["wg0"] = f(inputs["L0_ffn_w_gate"].T)
    sh["wu0"] = f(inputs["L0_ffn_w_up"].T)
    sh["wd0"] = f(inputs["L0_ffn_w_down"].T)
    W1 = np.array(inputs["L1_attn_W_in"])
    perm = np.concatenate([np.arange(0, 16, 2), np.arange(1, 16, 2), np.arange(16, 32)])
    Bcols = 2 * DI + perm
    Ccols = 2 * DI + DS + perm
    W1p = W1.copy()
    W1p[:, 2 * DI:2 * DI + DS] = W1[:, Bcols]
    W1p[:, 2 * DI + DS:2 * DI + 2 * DS] = W1[:, Ccols]
    sh["w1_in"] = f(W1p)
    sh["ln1_w"] = f(inputs["L1_attn_ln_w"].reshape(8, 128).T)
    sh["ln1_b"] = f(inputs["L1_attn_ln_b"].reshape(8, 128).T)
    sh["dt_bias"] = f(inputs["L1_attn_dt_bias"].reshape(-1, 1))
    sh["a_log"] = f(inputs["L1_attn_A_log"].reshape(-1, 1))
    sh["d_row"] = f(inputs["L1_attn_D"].reshape(1, -1))
    sh["rms_w"] = f(inputs["L1_attn_rms_w"].reshape(16, 128).T)
    sh["w1_out"] = f(inputs["L1_attn_W_out"])
    sh["rtr"] = f(inputs["L1_ffn_router_w"].T.reshape(8, 128, NE)
                  .transpose(1, 0, 2).reshape(128, 8 * NE))
    sh["mgw"] = f(inputs["L1_ffn_gate_w"])
    sh["mgb"] = f(inputs["L1_ffn_gate_b"].reshape(NE, 16, 128)
                  .transpose(2, 0, 1).reshape(128, NE * 16))
    sh["muw"] = f(inputs["L1_ffn_up_w"])
    sh["mub"] = f(inputs["L1_ffn_up_b"].reshape(NE, 16, 128)
                  .transpose(2, 0, 1).reshape(128, NE * 16))
    sh["mdw"] = f(inputs["L1_ffn_down_w"])
    sh["mdb"] = f(inputs["L1_ffn_down_b"])
    sh["ident"] = np.eye(128, dtype=np.float32)
    Tt = inputs["L0_attn_x"].shape[1]
    t = np.arange(Tt, dtype=np.float64)
    inv_freq = 1.0 / 10000.0 ** (np.arange(0, 16, 2, dtype=np.float64) / 16.0)
    ang = inv_freq[:, None] * t[None, :]
    cos = np.cos(ang).astype(np.float32)
    sin = np.sin(ang).astype(np.float32)
    cs1h = np.ones((32, Tt), np.float32)
    cs2h = np.zeros((32, Tt), np.float32)
    cs1h[0:8] = cos; cs1h[8:16] = cos
    cs2h[0:8] = -sin; cs2h[8:16] = sin
    sh["cs1"] = f(np.concatenate([cs1h, cs1h], 0))
    sh["cs2"] = f(np.concatenate([cs2h, cs2h], 0))
    swp = np.zeros((64, 64), np.float32)
    for blk in (0, 32):
        for i in range(8):
            swp[blk + 8 + i, blk + i] = 1.0
            swp[blk + i, blk + 8 + i] = 1.0
        for i in range(16, 32):
            swp[blk + i, blk + i] = 1.0
    sh["swap64"] = swp
    sel64 = np.zeros((HN, HN * 64), np.float32)
    for h in range(HN):
        sel64[h, h * 64:(h + 1) * 64] = 1.0
    sh["sel64"] = sel64
    ut = np.triu(np.ones((64, 64), np.float32))
    sh["utmask"] = f(np.concatenate([ut, ut], 0))
    sh["ident64b"] = np.eye(64, dtype=np.float32)
    sh["ones128"] = np.ones((1, 128), np.float32)
    sh["onescol"] = np.ones((128, 1), np.float32)
    x = inputs["L0_attn_x"]
    xfm = f(np.transpose(x, (0, 2, 1)))
    nbt = x.shape[0]
    nbc = nbt // NCORES
    per_core = [f(xfm[c * nbc:(c + 1) * nbc]) for c in range(NCORES)]
    return sh, per_core



def build_core(dev=False, stages=3, Tdev=2048, TMcap=1024):
    global T
    T = Tdev
    nc = bass.Bass("TRN2", target_bir_lowering=False, debug=False)
    stack = ExitStack()
    k = K(nc, stack)
    NTt = min(512, T)
    SL = min(512, T)
    TM = min(TMcap, T)
    NJ = T // NTt
    NS = T // SL
    NCP = SL // 128
    DI_ = DI

    x_in = nc.dram_tensor("x", [NB, D, T], F32, kind="ExternalInput").ap()
    out = nc.dram_tensor("out", [NB, T, D], F32, kind="ExternalOutput").ap()

    def win(name, shape, dt=F32):
        return nc.dram_tensor(name, list(shape), dt, kind="ExternalInput").ap()
    w0_in = win("w0_in", [D, D]); w0_a = win("w0_a", [D, D]); w0_b = win("w0_b", [D, D])
    w0_v = win("w0_v", [D, D]); w0_o = win("w0_o", [D, D])
    b0_a = win("b0_a", [128, 8]); b0_b = win("b0_b", [128, 8]); b0_v = win("b0_v", [128, 8])
    wg0 = win("wg0", [D, F0], I32); wu0 = win("wu0", [D, F0], I32)
    wd0 = win("wd0", [F0, D], I32)
    w1_in = win("w1_in", [D, 4192]); ln1_w = win("ln1_w", [128, 8]); ln1_b = win("ln1_b", [128, 8])
    dt_bias = win("dt_bias", [HN, 1]); a_log = win("a_log", [HN, 1])
    d_row = win("d_row", [1, HN])
    rms_w = win("rms_w", [128, 16]); w1_out = win("w1_out", [DI_, D])
    rtr = win("rtr", [128, 8 * NE])
    mgw = win("mgw", [NE, D, DFF]); mgb = win("mgb", [128, NE * 16])
    muw = win("muw", [NE, D, DFF]); mub = win("mub", [128, NE * 16])
    mdw = win("mdw", [NE, DFF, D]); mdb = win("mdb", [NE, D])
    ident = win("ident", [128, 128]); cs1 = win("cs1", [64, T]); cs2 = win("cs2", [64, T])
    swap64 = win("swap64", [64, 64])
    sel64 = win("sel64", [HN, HN * 64]); utmask = win("utmask", [128, 64])
    ident64b = win("ident64b", [64, 64]); ones128 = win("ones128", [1, 128])
    onescol = win("onescol", [128, 1])
    skind = "ExternalOutput" if dev else "Internal"
    s0_d = nc.dram_tensor("s0_d", [NB, D, T], F32, kind=skind).ap()
    x1_d = nc.dram_tensor("x1_d", [NB, D, T], F32, kind=skind).ap()
    s1_d = nc.dram_tensor("s1_d", [NB, D, T], F32, kind=skind).ap()
    if dev:
        xn_dbg = nc.dram_tensor("xn_dbg", [NB, D, T], F32, kind="ExternalOutput").ap()
        y_dbg = nc.dram_tensor("y_dbg", [NB, DI_, T], F32, kind="ExternalOutput").ap()
        g1_dbg = nc.dram_tensor("g1_dbg", [NB, D, T], F32, kind="ExternalOutput").ap()
        ga_dbg = nc.dram_tensor("ga_dbg", [NB, T, D], F32, kind="ExternalOutput").ap()
        p_dbg = nc.dram_tensor("p_dbg", [NB, T, NE], F32, kind="ExternalOutput").ap()

    XN = Arena(k, "XN", 8 * T, BF16)
    AB = Arena(k, "AB", max(8 * T, 16 * TM, 32 * NTt, 24 * SL + DI + 1024), BF16)
    CB = Arena(k, "CB", max(8 * T, 4 * DI_ + 16 * SL, 8 * TM), BF16)
    WB = Arena(k, "WB", 8192, BF16)
    FA = Arena(k, "FA", max(4 * T, 16 * NTt, 16 * SL, 8 * TM), F32)
    SM = Arena(k, "SM", 7040, F32)
    CSTF = Arena(k, "CSTF", 2436, F32)
    CSTB = Arena(k, "CSTB", 4292, BF16)
    psum = [stack.enter_context(nc.psum_tensor(f"pb{i}", [128, 512], F32)) for i in range(8)]
    ptt = [TT(f"pb{i}") for i in range(8)]

    c_ld = k.chan("c_ld")
    c_ldb = k.chan("c_ldb")
    R_id = CSTF.carve("ident", 128, 128, c0=0)
    R_swap = CSTF.carve("swap", 64, 64, c0=128)
    R_ut = CSTF.carve("ut", 128, 64, c0=192)
    R_sel64 = CSTF.carve("sel64", HN, HN * 64, c0=256)
    R_ones = CSTF.carve("ones", 1, 128, c0=256 + HN * 64)
    R_onec = CSTF.carve("onec", 128, 1, c0=384 + HN * 64)
    assert 385 + HN * 64 <= 2436
    k.dma("sp", c_ld, R_id.ap(), ident, writes=[R_id.tt])
    k.dma("sp", c_ld, R_swap.ap(), swap64, writes=[R_swap.tt])
    k.dma("sp", c_ld, R_ut.ap(), utmask, writes=[R_ut.tt])
    k.dma("sp", c_ld, R_sel64.ap(), sel64, writes=[R_sel64.tt])
    k.dma("sp", c_ld, R_ones.ap(), ones128, writes=[R_ones.tt])
    k.dma("sp", c_ld, R_onec.ap(), onescol, writes=[R_onec.tt])
    R_rtrb = CSTB.carve("rtrb", 128, 8 * NE, c0=0)
    R_onesb = CSTB.carve("onesb", 1, 128, c0=64)
    R_id64b = CSTB.carve("id64b", 64, 64, c0=192)
    R_dident = CSTB.carve("dident", 128, HN * 64, c0=256)
    R_mt = [CSTB.carve(f"mt{i}", 128, 64, c0=256 + HN * 64 + i * 64) for i in range(2)]
    R_bkd = [CSTB.carve(f"bkd{i}", 128, 32, c0=384 + HN * 64 + i * 32) for i in range(2)]
    R_csA = CSTB.carve("csA", 128, SL, c0=448 + HN * 64)
    R_csB = CSTB.carve("csB", 32, SL, c0=448 + HN * 64 + SL)
    R_onecb = CSTB.carve("onecb", 128, 1, c0=448 + HN * 64 + 2 * SL)
    assert 449 + HN * 64 + 2 * SL <= 3524
    k.dma("pool", c_ldb, R_rtrb.ap(), rtr, writes=[R_rtrb.tt])
    k.dma("pool", c_ldb, R_onesb.ap(), ones128, writes=[R_onesb.tt])
    k.dma("pool", c_ldb, R_onecb.ap(), onescol, writes=[R_onecb.tt])
    k.dma("pool", c_ldb, R_id64b.ap(), ident64b, writes=[R_id64b.tt])
    smp = [6144]
    def smpark(name, rows, cols, r0=0):
        r = SM.carve(name, rows, cols, r0=r0, c0=smp[0])
        smp[0] += cols
        assert smp[0] <= 7040, smp[0]
        return r
    R_b0a = smpark("b0a", 128, 8); R_b0b = smpark("b0b", 128, 8); R_b0v = smpark("b0v", 128, 8)
    R_ln1w = smpark("ln1w", 128, 8); R_ln1b = smpark("ln1b", 128, 8)
    R_rmsw = smpark("rmsw", 128, 16)
    R_dtb = smpark("dtb", HN, 1, r0=64); R_alog = smpark("alog", HN, 1, r0=64)
    R_acol = smpark("acol", HN, 1, r0=64)
    R_d64 = smpark("d64", 64, HN)
    R_drow = smpark("drow", 1, HN)
    R_mgb = smpark("mgb", 128, NE * 16); R_mub = smpark("mub", 128, NE * 16)
    for src, reg in ((b0_a, R_b0a), (b0_b, R_b0b), (b0_v, R_b0v),
                     (ln1_w, R_ln1w), (ln1_b, R_ln1b)):
        k.dma("sp", c_ld, reg.ap(), src, writes=[reg.tt])
    k.dma("sp", c_ld, R_rmsw.ap(), rms_w, writes=[R_rmsw.tt])
    k.dma("sp", c_ld, R_dtb.ap(), dt_bias, writes=[R_dtb.tt])
    k.dma("sp", c_ld, R_alog.ap(), a_log, writes=[R_alog.tt])
    k.dma("sp", c_ld, R_drow.ap(), d_row, writes=[R_drow.tt])
    k.dma("sp", c_ld, R_mgb.ap(), mgb, writes=[R_mgb.tt])
    k.dma("sp", c_ld, R_mub.ap(), mub, writes=[R_mub.tt])
    R_mdbblk = SM.carve("mdbblk", NE, D, r0=64, c0=512)
    k.dma("sp", c_ld, R_mdbblk.ap(), mdb, writes=[R_mdbblk.tt])
    for _reg in (R_id, R_swap, R_ut, R_sel64, R_ones, R_onec, R_b0a, R_b0b, R_b0v,
                 R_ln1w, R_ln1b, R_rmsw, R_dtb, R_alog, R_drow, R_mgb, R_mub,
                 R_mdbblk):
        _reg.tt.w["c_ld"] = c_ld.cnt
    for _reg in (R_rtrb, R_onesb, R_onecb, R_id64b):
        _reg.tt.w["c_ldb"] = c_ldb.cnt

    k.op("act", lambda e: e.activation(out=R_acol.ap(), in_=R_alog.ap(), func=AF.Exp),
         reads=[R_alog.tt], writes=[R_acol.tt])
    k.op("dve", lambda e: e.tensor_scalar_mul(R_acol.ap(), R_acol.ap(), -1.0),
         reads=[R_acol.tt], writes=[R_acol.tt])
    k.ops("pe", [lambda e: e.matmul(psum[7][0:64, 0:HN], R_ones.ap((0, 1), (0, 64)),
                                    R_drow.ap(), start=True, stop=True)],
          reads=[R_ones.tt, R_drow.tt], writes=[ptt[7]])
    k.op("dve", lambda e: e.tensor_copy(R_d64.ap(), psum[7][0:64, 0:HN]),
         reads=[ptt[7]], writes=[R_d64.tt])
    for h in range(HN):
        k.op("dve", lambda e, h=h: e.tensor_scalar(
            R_dident.ap((0, 64), (h * 64, h * 64 + 64)), R_id64b.ap(),
            R_d64.ap((0, 64), (h, h + 1)), None, ALU.mult),
            reads=[R_id64b.tt, R_d64.tt], writes=[R_dident.tt])
        k.op("dve", lambda e, h=h: e.tensor_copy(
            R_dident.ap((64, 128), (h * 64, h * 64 + 64)),
            R_dident.ap((0, 64), (h * 64, h * 64 + 64))),
            reads=[R_dident.tt], writes=[R_dident.tt])

    def ln_fm(src_regs, src_aps, dst_ap_fn, dst_tts, nt, affine=None, eps=1e-5,
              ones_x=None):
        if ones_x is None:
            ones_x = R_onec
        pS, pQ = psum[6], psum[7]
        R_sq = SM.carve("lnsq", 128, 512, c0=0)
        mean = SM.carve("ln_mean", 1, 512, c0=512)
        rstd = SM.carve("ln_rstd", 1, 512, c0=1024)
        msq = SM.carve("ln_msq", 1, 512, c0=1536)
        var = SM.carve("ln_var", 1, 512, c0=2048)
        ems = []
        for c in range(8):
            ems.append(lambda e, c=c: e.matmul(pS[0:1, 0:nt], ones_x.ap(),
                                               src_aps[c], start=(c == 0), stop=(c == 7)))
        k.ops("pe", ems, reads=[ones_x.tt] + [r.tt for r in src_regs], writes=[ptt[6]])
        for c in range(8):
            k.op("act", lambda e, c=c: e.activation(out=R_sq.ap((0, 128), (0, nt)),
                 in_=src_aps[c], func=AF.Square),
                 reads=[src_regs[c].tt], writes=[R_sq.tt])
            k.ops("pe", [lambda e, c=c: e.matmul(pQ[0:1, 0:nt], R_onec.ap(),
                         R_sq.ap((0, 128), (0, nt)), start=(c == 0), stop=(c == 7))],
                  reads=[R_onec.tt, R_sq.tt], writes=[ptt[7]])
        k.op("dve", lambda e: e.tensor_scalar_mul(mean.ap((0, 1), (0, nt)), pS[0:1, 0:nt],
             1.0 / D), reads=[ptt[6]], writes=[mean.tt])
        k.op("act", lambda e: e.activation(out=msq.ap((0, 1), (0, nt)),
             in_=mean.ap((0, 1), (0, nt)), func=AF.Square), reads=[mean.tt], writes=[msq.tt])
        k.op("dve", lambda e: e.scalar_tensor_tensor(var.ap((0, 1), (0, nt)), pQ[0:1, 0:nt],
             1.0 / D, msq.ap((0, 1), (0, nt)), ALU.mult, ALU.subtract),
             reads=[ptt[7], msq.tt], writes=[var.tt])
        k.op("dve", lambda e: e.tensor_scalar_add(var.ap((0, 1), (0, nt)),
             var.ap((0, 1), (0, nt)), float(eps)), reads=[var.tt], writes=[var.tt])
        k.op("act", lambda e: e.activation(out=msq.ap((0, 1), (0, nt)),
             in_=var.ap((0, 1), (0, nt)), func=AF.Ln),
             reads=[var.tt], writes=[msq.tt])
        k.op("act", lambda e: e.activation(out=rstd.ap((0, 1), (0, nt)),
             in_=msq.ap((0, 1), (0, nt)), func=AF.Exp, scale=-0.5),
             reads=[msq.tt], writes=[rstd.tt])
        k.ops("pe", [lambda e: e.matmul(pS[0:128, 0:nt], R_ones.ap(),
                     mean.ap((0, 1), (0, nt)), start=True, stop=True)],
              reads=[R_ones.tt, mean.tt], writes=[ptt[6]])
        k.ops("pe", [lambda e: e.matmul(pQ[0:128, 0:nt], R_ones.ap(),
                     rstd.ap((0, 1), (0, nt)), start=True, stop=True)],
              reads=[R_ones.tt, rstd.tt], writes=[ptt[7]])
        R_t = R_sq
        for c in range(8):
            k.op("dve", lambda e, c=c: e.tensor_sub(R_t.ap((0, 128), (0, nt)), src_aps[c],
                 pS[0:128, 0:nt]), reads=[src_regs[c].tt, ptt[6]], writes=[R_t.tt])
            if affine is None:
                k.op("dve", lambda e, c=c: e.tensor_mul(dst_ap_fn(c),
                     R_t.ap((0, 128), (0, nt)), pQ[0:128, 0:nt]),
                     reads=[R_t.tt, ptt[7]], writes=[dst_tts[c]])
            else:
                aw, ab_ = affine
                k.op("dve", lambda e, c=c: e.tensor_mul(R_t.ap((0, 128), (0, nt)),
                     R_t.ap((0, 128), (0, nt)), pQ[0:128, 0:nt]),
                     reads=[R_t.tt, ptt[7]], writes=[R_t.tt])
                k.op("dve", lambda e, c=c: e.tensor_scalar(dst_ap_fn(c),
                     R_t.ap((0, 128), (0, nt)), aw.ap((0, 128), (c, c + 1)),
                     ab_.ap((0, 128), (c, c + 1)), ALU.mult, ALU.add),
                     reads=[R_t.tt, aw.tt, ab_.tt], writes=[dst_tts[c]])

    def wgroup(pairs):
        ch = k.chan("ch_wb")
        for slot, src_ap in pairs:
            k.dma("pool", ch, slot.ap(), src_ap, writes=[slot.tt])
        for slot, _ in pairs:
            slot.tt.w["ch_wb"] = ch.cnt

    wseq = [0]

    def wslots(width, chunks):
        par = wseq[0] % 2
        wseq[0] += 1
        reg = WB.carve(f"wbp{par}", 128, 4096, c0=par * 4096)
        ch = k.chan(f"ch_wb{par}")
        off = 0
        slots = []
        for src2d, n in chunks:
            k.dma("pool", ch,
                  reg.ap((0, 128), (off, off + n * width)).rearrange(
                      "p (kk c) -> p kk c", c=width),
                  src2d.rearrange("(kk p) c -> p kk c", p=128),
                  writes=[reg.tt])
            for i in range(n):
                slots.append(Reg(reg.t, 0, 128, reg.c0 + off + i * width, width, reg.tt))
            off += n * width
        assert off <= 4096
        reg.tt.w[ch.name] = ch.cnt
        return slots

    def l0(b):
        for A in (XN, AB, CB, WB, FA, SM):
            A.new_phase()
        R_xn = XN.carve("xn", 128, 8 * T, c0=0)
        xnap = lambda c, t0, t1: R_xn.ap((0, 128), (c * T + t0, c * T + t1))
        R_lnx = FA.carve("lnx", 128, 8 * NTt, c0=0)
        ch_x = k.chan("ch_x")
        for j in range(NJ):
            t0 = j * NTt
            k.dma("sp", ch_x, R_lnx.ap().rearrange("p (c t) -> p c t", c=8),
                  x_in[b, :, t0:t0 + NTt].rearrange("(c p) t -> p c t", p=128),
                  writes=[R_lnx.tt])
            ln_fm([R_lnx] * 8,
                  [R_lnx.ap((0, 128), (c * NTt, (c + 1) * NTt)) for c in range(8)],
                  lambda c, t0=t0: xnap(c, t0, t0 + NTt), [R_xn.tt] * 8, NTt)
        if dev:
            ch_dbg = k.chan("ch_dbg")
            for c in range(8):
                k.dma("pool", ch_dbg, xn_dbg[b, c * 128:(c + 1) * 128, :], xnap(c, 0, T),
                      reads=[R_xn.tt])
        AB.new_phase()
        R_xp = AB.carve("xp", 128, 8 * T, c0=0)
        xpap = lambda c, t0, t1: R_xp.ap((0, 128), (c * T + t0, c * T + t1))
        for cg in range(2):
            slots = wslots(512, [(w0_in[:, cg * 512:(cg + 1) * 512], 8)])
            for j in range(NJ):
                t0 = j * NTt
                for mi in range(4):
                    m = cg * 4 + mi
                    bank = (cg * 4 + mi) % 6
                    k.ops("pe", [
                        (lambda e, kk=kk, mi=mi, bank=bank, t0=t0, slots=slots: e.matmul(
                            psum[bank][:, 0:NTt],
                            slots[kk].ap((0, 128), (mi * 128, (mi + 1) * 128)),
                            xnap(kk, t0, t0 + NTt), start=(kk == 0), stop=(kk == 7)))
                        for kk in range(8)],
                        reads=[s.tt for s in slots] + [R_xn.tt], writes=[ptt[bank]])
                    k.op("act", lambda e, m=m, bank=bank, t0=t0: e.activation(
                        out=xpap(m, t0, t0 + NTt), in_=psum[bank][:, 0:NTt], func=AF.Silu),
                        reads=[ptt[bank]], writes=[R_xp.tt])
        WB.new_phase(); FA.new_phase(); CB.new_phase()
        R_cell = CB.carve("cell", 128, 8 * T, c0=0)
        cellap = lambda c, t0, t1: R_cell.ap((0, 128), (c * T + t0, c * T + t1))
        R_al = FA.carve("al", 128, T, c0=0)
        R_bvr = FA.carve("bvr", 128, T, c0=T)
        R_bt = FA.carve("bt", 128, T, c0=2 * T)
        R_vt = FA.carve("vt", 128, T, c0=3 * T)
        for m in range(8):
            f0c = m * 128
            sl24 = wslots(128, [(w0_a[:, f0c:f0c + 128], 8),
                                (w0_b[:, f0c:f0c + 128], 8),
                                (w0_v[:, f0c:f0c + 128], 8)])
            wsl = {}
            for wi, wnm in enumerate(("a", "b", "v")):
                for kk in range(8):
                    wsl[(wnm, kk)] = sl24[wi * 8 + kk]
            if True:
                mc = 0
                for (wnm, bias_reg, dst, fn) in (("a", R_b0a, R_al, AF.Sigmoid),
                                                 ("b", R_b0b, R_bt, AF.Sigmoid),
                                                 ("v", R_b0v, R_vt, AF.Tanh)):
                    bank = {"a": 0, "b": 1, "v": 2}[wnm]
                    for j in range(NJ):
                        t0 = j * NTt
                        k.ops("pe", [
                            (lambda e, kk=kk, bank=bank, mc=mc, t0=t0, wnm=wnm, wsl=wsl: e.matmul(
                                psum[bank][:, 0:NTt],
                                wsl[(wnm, kk)].ap((0, 128), (mc, mc + 128)),
                                xpap(kk, t0, t0 + NTt), start=(kk == 0), stop=(kk == 7)))
                            for kk in range(8)],
                            reads=[wsl[(wnm, kk)].tt for kk in range(8)] + [R_xp.tt],
                            writes=[ptt[bank]])
                        k.op("act", lambda e, bank=bank, dst=dst, t0=t0, fn=fn, m=m,
                             br=bias_reg: e.activation(
                                out=dst.ap((0, 128), (t0, t0 + NTt)),
                                in_=psum[bank][:, 0:NTt], func=fn,
                                bias=br.ap((0, 128), (m, m + 1))),
                             reads=[ptt[bank], bias_reg.tt], writes=[dst.tt])
                k.op("dve", lambda e: e.tensor_mul(R_bvr.ap(), R_bt.ap(), R_vt.ap()),
                     reads=[R_bt.tt, R_vt.tt], writes=[R_bvr.tt])
                k.op("dve", lambda e: e.tensor_tensor_scan(R_bt.ap(), R_al.ap(),
                     R_bvr.ap(), 0.0, ALU.mult, ALU.add),
                     reads=[R_al.tt, R_bvr.tt], writes=[R_bt.tt])
                k.op("dve", lambda e, m=m: e.tensor_copy(cellap(m, 0, T), R_bt.ap()),
                     reads=[R_bt.tt], writes=[R_cell.tt])
        for m in range(8):
            k.op("act", lambda e, m=m: e.activation(out=R_vt.ap(), in_=cellap(m, 0, T),
                 func=AF.Silu), reads=[R_cell.tt], writes=[R_vt.tt])
            k.op("dve", lambda e, m=m: e.tensor_mul(cellap(m, 0, T), cellap(m, 0, T),
                 R_vt.ap()), reads=[R_cell.tt, R_vt.tt], writes=[R_cell.tt])
        WB.new_phase(); FA.new_phase()
        R_st0 = FA.carve("s0st", 128, 2 * NTt, c0=0)
        ch_s0 = k.chan("ch_s0")
        for cg in range(2):
            oslots = wslots(512, [(w0_o[:, cg * 512:(cg + 1) * 512], 8)])
            for mi in range(4):
                m = cg * 4 + mi
                for j in range(NJ):
                    t0 = j * NTt
                    bank = (cg * 4 + mi + j) % 6
                    sl_ = (j % 2) * NTt
                    k.ops("pe", [
                        (lambda e, kk=kk, mi=mi, bank=bank, t0=t0, oslots=oslots: e.matmul(
                            psum[bank][:, 0:NTt],
                            oslots[kk].ap((0, 128), (mi * 128, (mi + 1) * 128)),
                            cellap(kk, t0, t0 + NTt), start=(kk == 0), stop=(kk == 7)))
                        for kk in range(8)],
                        reads=[s.tt for s in oslots] + [R_cell.tt], writes=[ptt[bank]])
                    k.op("dve", lambda e, bank=bank, sl_=sl_: e.tensor_copy(
                        R_st0.ap((0, 128), (sl_, sl_ + NTt)), psum[bank][:, 0:NTt]),
                        reads=[ptt[bank]], writes=[R_st0.tt])
                    k.dma("sp", ch_s0, s0_d[b, m * 128:(m + 1) * 128, t0:t0 + NTt],
                          R_st0.ap((0, 128), (sl_, sl_ + NTt)), reads=[R_st0.tt])
        AB.new_phase(); WB.new_phase(); FA.new_phase()
        tt_s0 = TT("s0_dram"); tt_s0.w["ch_s0"] = ch_s0.cnt
        ch_x1 = k.chan("ch_x1")
        R_hf = AB.carve("hf", 128, 24 * NTt, c0=0)
        R_g0 = AB.carve("g0", 128, 8 * NTt, c0=24 * NTt)
        R_cx = FA.carve("cx", 128, 8 * NTt, c0=0)
        R_cs0 = FA.carve("cs0", 128, 8 * NTt, c0=8 * NTt)
        for j in range(NJ):
            t0 = j * NTt
            for grp in range(6):
                gslots = wslots(512, [(wg0[:, grp * 512:(grp + 1) * 512], 8)])
                for mi in range(4):
                    m = grp * 4 + mi
                    bank = (grp * 4 + mi) % 6
                    k.ops("pe", [
                        (lambda e, kk=kk, mi=mi, t0=t0, bank=bank, gslots=gslots: e.matmul(
                            psum[bank][:, 0:NTt],
                            gslots[kk].ap((0, 128), (mi * 128, (mi + 1) * 128)),
                            xnap(kk, t0, t0 + NTt), start=(kk == 0), stop=(kk == 7)))
                        for kk in range(8)],
                        reads=[s.tt for s in gslots] + [R_xn.tt], writes=[ptt[bank]])
                    k.op("act", lambda e, m=m, bank=bank: e.activation(
                        out=R_hf.ap((0, 128), (m * NTt, (m + 1) * NTt)),
                        in_=psum[bank][:, 0:NTt], func=AF.Silu, scale=S127),
                        reads=[ptt[bank]], writes=[R_hf.tt])
                uslots = wslots(512, [(wu0[:, grp * 512:(grp + 1) * 512], 8)])
                for mi in range(4):
                    m = grp * 4 + mi
                    bank = (grp * 4 + mi) % 6
                    k.ops("pe", [
                        (lambda e, kk=kk, mi=mi, t0=t0, bank=bank, uslots=uslots: e.matmul(
                            psum[bank][:, 0:NTt],
                            uslots[kk].ap((0, 128), (mi * 128, (mi + 1) * 128)),
                            xnap(kk, t0, t0 + NTt), start=(kk == 0), stop=(kk == 7)))
                        for kk in range(8)],
                        reads=[s.tt for s in uslots] + [R_xn.tt], writes=[ptt[bank]])
                    k.op("dve", lambda e, m=m, bank=bank: e.scalar_tensor_tensor(
                        R_hf.ap((0, 128), (m * NTt, (m + 1) * NTt)), psum[bank][:, 0:NTt],
                        S127, R_hf.ap((0, 128), (m * NTt, (m + 1) * NTt)),
                        ALU.mult, ALU.mult),
                        reads=[ptt[bank], R_hf.tt], writes=[R_hf.tt])
            for dgrp in range(2):
                for kh in range(4):
                    dslots = wslots(512, [(wd0[kh * 768:(kh + 1) * 768,
                                               dgrp * 512:(dgrp + 1) * 512], 6)])
                    for mo in range(4):
                        k.ops("pe", [
                            (lambda e, kk=kk, kh=kh, mo=mo, dslots=dslots: e.matmul(
                                psum[mo][:, 0:NTt],
                                dslots[kk].ap((0, 128), (mo * 128, (mo + 1) * 128)),
                                R_hf.ap((0, 128), ((kh * 6 + kk) * NTt,
                                                   (kh * 6 + kk + 1) * NTt)),
                                start=(kh == 0 and kk == 0), stop=(kh == 3 and kk == 5)))
                            for kk in range(6)],
                            reads=[s.tt for s in dslots] + [R_hf.tt], writes=[ptt[mo]])
                for mo in range(4):
                    dmo = dgrp * 4 + mo
                    k.op("act", lambda e, mo=mo, dmo=dmo: e.activation(
                        out=R_g0.ap((0, 128), (dmo * NTt, (dmo + 1) * NTt)),
                        in_=psum[mo][:, 0:NTt], func=AF.Silu, scale=S127),
                        reads=[ptt[mo]], writes=[R_g0.tt])
            k.dma("sp", ch_x, R_cx.ap().rearrange("p (c t) -> p c t", c=8),
                  x_in[b, :, t0:t0 + NTt].rearrange("(c p) t -> p c t", p=128),
                  writes=[R_cx.tt])
            k.dma("sp", k.chan("ch_s0r"), R_cs0.ap().rearrange("p (c t) -> p c t", c=8),
                  s0_d[b, :, t0:t0 + NTt].rearrange("(c p) t -> p c t", p=128),
                  reads=[tt_s0], writes=[R_cs0.tt])
            for c in range(8):
                k.op("dve", lambda e, c=c: e.tensor_mul(
                    R_cs0.ap((0, 128), (c * NTt, (c + 1) * NTt)),
                    R_cs0.ap((0, 128), (c * NTt, (c + 1) * NTt)),
                    R_g0.ap((0, 128), (c * NTt, (c + 1) * NTt))),
                    reads=[R_cs0.tt, R_g0.tt], writes=[R_cs0.tt])
                k.op("dve", lambda e, c=c: e.tensor_add(
                    R_cx.ap((0, 128), (c * NTt, (c + 1) * NTt)),
                    R_cx.ap((0, 128), (c * NTt, (c + 1) * NTt)),
                    R_cs0.ap((0, 128), (c * NTt, (c + 1) * NTt))),
                    reads=[R_cx.tt, R_cs0.tt], writes=[R_cx.tt])
                k.dma("sp", ch_x1, x1_d[b, c * 128:(c + 1) * 128, t0:t0 + NTt],
                      R_cx.ap((0, 128), (c * NTt, (c + 1) * NTt)), reads=[R_cx.tt])
            ln_fm([R_cx] * 8,
                  [R_cx.ap((0, 128), (c * NTt, (c + 1) * NTt)) for c in range(8)],
                  lambda c, t0=t0: xnap(c, t0, t0 + NTt), [R_xn.tt] * 8, NTt)
        tt_x1 = TT("x1_dram")
        tt_x1.w["ch_x1"] = ch_x1.cnt
        return tt_x1

    def l1a(b):
        for A in (AB, CB, WB, FA, SM):
            A.new_phase()
        R_xn = XN.live["xn"]
        xnap = lambda c, t0, t1: R_xn.ap((0, 128), (c * T + t0, c * T + t1))
        R_rawbc = SM.carve("rawbc", 64, SL, r0=0, c0=2560)
        R_ropB = SM.carve("ropB", 32, SL, r0=64, c0=2560)
        R_dec = SM.carve("dec", 32, SL, r0=0, c0=3072)
        R_a = SM.carve("a", 32, SL, r0=32, c0=3072)
        R_ropC = SM.carve("ropC", 32, SL, r0=64, c0=3072)
        R_acs = SM.carve("acs", 32, SL, r0=0, c0=3584)
        R_dtm = SM.carve("dtm", 32, SL, r0=64, c0=3584)
        R_eacs = SM.carve("eacs", 32, SL, r0=0, c0=4096)
        R_dta = SM.carve("dta", 32, SL, r0=64, c0=4096)
        R_csum = SM.carve("csum", 32, 8, r0=0, c0=4608)
        R_ecs = SM.carve("ecs", 32, 8, r0=0, c0=4616)
        R_cks = [SM.carve(f"cks{i}", 32, 128, r0=32, c0=3584 + i * 128) for i in range(2)]
        R_cke = [SM.carve(f"cke{i}", 32, 64, r0=32, c0=4096 + i * 64) for i in range(2)]
        R_stsb = [[SM.carve(f"stsb{i}_{ci}", 32, 64, r0=32, c0=4608 + (i * 2 + ci) * 64)
                   for ci in range(2)] for i in range(2)]
        R_ebc = SM.carve("ebc", 32, 256, r0=32, c0=4864)
        R_dta2 = SM.carve("dta2", 32, SL, r0=64, c0=4608)
        R_dt = SM.carve("dt", 32, SL, r0=64, c0=5120)
        R_sal = [SM.carve(f"sal{h}", HN, 64, r0=32, c0=512 + h * 64) for h in range(HN)]
        tks = 5632
        R_nacs = [SM.carve(f"nacs{i}", 128, 32, c0=tks + i * 32) for i in range(2)]
        R_dectk = [SM.carve(f"dectk{i}", 128, 32, c0=tks + 64 + i * 32) for i in range(2)]
        R_dttk = [SM.carve(f"dttk{i}", 128, 32, c0=tks + 128 + i * 32) for i in range(2)]
        R_bktk = [SM.carve(f"bktk{i}", 128, 32, c0=tks + 192 + i * 32) for i in range(2)]
        R_scm = [SM.carve(f"scm{i}", 128, 64, c0=tks + 256 + i * 64) for i in range(2)]
        R_mdf = [SM.carve(f"mdf{i}", 128, 64, c0=tks + 384 + i * 64) for i in range(2)]
        assert tks + 512 <= 6144
        R_h1 = AB.carve("h1", 128, 8 * SL, c0=0)
        R_sz = AB.carve("sz", 128, 16 * SL, c0=8 * SL)
        R_mta = AB.carve("mta", 128, DI_, c0=24 * SL)
        R_bka = AB.carve("bka", 128, 1024, c0=24 * SL + DI_)
        R_mex = [SM.carve(f"mex{i}", 128, 64, c0=6528 + i * 64) for i in range(4)]
        R_xst = CB.carve("xst", 128, 4 * DI_, c0=0)
        R_y = CB.carve("y", 128, 16 * SL, c0=4 * DI_)
        R_yg = FA.carve("yg", 128, 16 * SL, c0=0)
        NCH = SL // CH
        ch_s1 = k.chan("ch_s1")
        ch_cs = k.chan("ch_cs")
        for h in range(HN):
            k.op("dve", lambda e, h=h: e.memset(R_sal[h].ap(), 0.0), writes=[R_sal[h].tt])
        for sl in range(NS):
            u0 = sl * SL
            k.dma("pool", ch_cs, R_csA.ap((0, 32), (0, SL)), cs1[0:32, u0:u0 + SL],
                  writes=[R_csA.tt])
            k.dma("pool", ch_cs, R_csA.ap((32, 64), (0, SL)), cs1[32:64, u0:u0 + SL],
                  writes=[R_csA.tt])
            k.dma("pool", ch_cs, R_csA.ap((64, 96), (0, SL)), cs2[0:32, u0:u0 + SL],
                  writes=[R_csA.tt])
            k.dma("pool", ch_cs, R_csB.ap((0, 32), (0, SL)), cs2[32:64, u0:u0 + SL],
                  writes=[R_csB.tt])
            R_csA.tt.w["ch_cs"] = ch_cs.cnt
            R_csB.tt.w["ch_cs"] = ch_cs.cnt
            ln_fm([R_xn] * 8, [xnap(c, u0, u0 + SL) for c in range(8)],
                  lambda c: R_h1.ap((0, 128), (c * SL, (c + 1) * SL)), [R_h1.tt] * 8,
                  SL, affine=(R_ln1w, R_ln1b), eps=1e-6, ones_x=R_onecb)
            for zg in range(4):
                zsl = wslots(512, [(w1_in[:, zg * 512:(zg + 1) * 512], 8)])
                for mi in range(4):
                    m = zg * 4 + mi
                    bank = (zg * 4 + mi) % 6
                    k.ops("pe", [
                        (lambda e, kk=kk, mi=mi, bank=bank, zsl=zsl: e.matmul(
                            psum[bank][:, 0:SL],
                            zsl[kk].ap((0, 128), (mi * 128, (mi + 1) * 128)),
                            R_h1.ap((0, 128), (kk * SL, (kk + 1) * SL)),
                            start=(kk == 0), stop=(kk == 7)))
                        for kk in range(8)],
                        reads=[s.tt for s in zsl] + [R_h1.tt], writes=[ptt[bank]])
                    k.op("act", lambda e, m=m, bank=bank: e.activation(
                        out=R_sz.ap((0, 128), (m * SL, (m + 1) * SL)),
                        in_=psum[bank][:, 0:SL], func=AF.Silu),
                        reads=[ptt[bank]], writes=[R_sz.tt])
            for xg in range(4):
                xsl = wslots(512, [(w1_in[:, DI_ + xg * 512:DI_ + (xg + 1) * 512], 8)])
                for tc in range(NCP):
                    bank = 2 + (xg % 2) * 2 + (tc % 2)
                    k.ops("pe", [
                        (lambda e, kk=kk, tc=tc, bank=bank, xsl=xsl: e.matmul(
                            psum[bank][:, 0:512],
                            R_h1.ap((0, 128), (kk * SL + tc * 128,
                                               kk * SL + (tc + 1) * 128)),
                            xsl[kk].ap(), start=(kk == 0), stop=(kk == 7)))
                        for kk in range(8)],
                        reads=[s.tt for s in xsl] + [R_h1.tt], writes=[ptt[bank]])
                    k.op("act", lambda e, tc=tc, xg=xg, bank=bank: e.activation(
                        out=R_xst.ap((0, 128), (tc * DI_ + xg * 512,
                                                tc * DI_ + (xg + 1) * 512)),
                        in_=psum[bank][:, 0:512], func=AF.Silu),
                        reads=[ptt[bank]], writes=[R_xst.tt])
            bsl = [CSTB.carve(f"w1b{kk}", 128, 96, c0=3524 + kk * 96) for kk in range(8)]
            wgroup([(bsl[kk], w1_in[kk * 128:(kk + 1) * 128, 4096:4192])
                    for kk in range(8)])
            k.ops("pe", [
                (lambda e, kk=kk: e.matmul(psum[5][0:96, 0:SL], bsl[kk].ap(),
                    R_h1.ap((0, 128), (kk * SL, (kk + 1) * SL)),
                    start=(kk == 0), stop=(kk == 7)))
                for kk in range(8)],
                reads=[s.tt for s in bsl] + [R_h1.tt], writes=[ptt[5]])
            k.op("act", lambda e: e.activation(out=R_rawbc.ap(), in_=psum[5][0:64, 0:SL],
                 func=AF.Silu), reads=[ptt[5]], writes=[R_rawbc.tt])
            k.op("dve", lambda e: e.tensor_scalar(R_dtm.ap(), psum[5][64:96, 0:SL],
                 R_dtb.ap(), 0.0, ALU.add, ALU.max), reads=[ptt[5], R_dtb.tt],
                 writes=[R_dtm.tt])
            k.op("act", lambda e: e.activation(out=R_dta.ap(), in_=psum[5][64:96, 0:SL],
                 func=AF.Abs, bias=R_dtb.ap()), reads=[ptt[5], R_dtb.tt], writes=[R_dta.tt])
            k.op("act", lambda e: e.activation(out=R_dta2.ap(), in_=R_dta.ap(),
                 func=AF.Exp, scale=-1.0), reads=[R_dta.tt], writes=[R_dta2.tt])
            k.op("act", lambda e: e.activation(out=R_dta.ap(), in_=R_dta2.ap(),
                 func=AF.Ln, bias=1.0), reads=[R_dta2.tt], writes=[R_dta.tt])
            k.op("dve", lambda e: e.tensor_add(R_dt.ap(), R_dtm.ap(), R_dta.ap()),
                 reads=[R_dtm.tt, R_dta.tt], writes=[R_dt.tt])
            k.ops("pe", [lambda e: e.matmul(psum[4][0:64, 0:SL], R_swap.ap(),
                         R_rawbc.ap(), start=True, stop=True)],
                  reads=[R_swap.tt, R_rawbc.tt], writes=[ptt[4]])
            k.op("dve", lambda e: e.tensor_mul(R_ropB.ap(), R_rawbc.ap((0, 32), (0, SL)),
                 R_csA.ap((0, 32), (0, SL))), reads=[R_rawbc.tt, R_csA.tt],
                 writes=[R_ropB.tt])
            k.op("dve", lambda e: e.tensor_mul(R_ropC.ap(), R_rawbc.ap((32, 64), (0, SL)),
                 R_csA.ap((32, 64), (0, SL))), reads=[R_rawbc.tt, R_csA.tt],
                 writes=[R_ropC.tt])
            k.op("dve", lambda e: e.tensor_tensor(R_dtm.ap(), psum[4][0:32, 0:SL],
                 R_csA.ap((64, 96), (0, SL)), ALU.mult), reads=[ptt[4], R_csA.tt],
                 writes=[R_dtm.tt])
            k.op("dve", lambda e: e.tensor_add(R_ropB.ap(), R_ropB.ap(), R_dtm.ap()),
                 reads=[R_ropB.tt, R_dtm.tt], writes=[R_ropB.tt])
            k.op("dve", lambda e: e.tensor_tensor(R_dta.ap((0, 32), (0, SL)),
                 psum[4][32:64, 0:SL], R_csB.ap((0, 32), (0, SL)), ALU.mult),
                 reads=[ptt[4], R_csB.tt], writes=[R_dta.tt])
            k.op("dve", lambda e: e.tensor_add(R_ropC.ap(), R_ropC.ap(),
                 R_dta.ap((0, 32), (0, SL))), reads=[R_ropC.tt, R_dta.tt],
                 writes=[R_ropC.tt])
            k.op("dve", lambda e: e.tensor_scalar(R_a.ap(), R_dt.ap(), R_acol.ap(), None,
                 ALU.mult), reads=[R_dt.tt, R_acol.tt], writes=[R_a.tt])
            for c in range(SL // CH):
                k.op("dve", lambda e, c=c: e.tensor_tensor_scan(
                    R_acs.ap((0, 32), (c * CH, (c + 1) * CH)),
                    R_a.ap((0, 32), (c * CH, (c + 1) * CH)),
                    R_a.ap((0, 32), (c * CH, (c + 1) * CH)), 0.0, ALU.add, ALU.bypass),
                    reads=[R_a.tt], writes=[R_acs.tt])
                k.op("dve", lambda e, c=c: e.reduce_sum(
                    R_csum.ap((0, 32), (c, c + 1)),
                    R_a.ap((0, 32), (c * CH, (c + 1) * CH)),
                    axis=AX.X), reads=[R_a.tt], writes=[R_csum.tt])
                k.op("act", lambda e, c=c: e.activation(
                    out=R_dec.ap((0, 32), (c * CH, (c + 1) * CH)),
                    in_=R_acs.ap((0, 32), (c * CH, (c + 1) * CH)), func=AF.Exp,
                    scale=-1.0, bias=R_csum.ap((0, 32), (c, c + 1))),
                    reads=[R_acs.tt, R_csum.tt], writes=[R_dec.tt])
            k.op("act", lambda e: e.activation(out=R_eacs.ap(), in_=R_acs.ap(),
                 func=AF.Exp), reads=[R_acs.tt], writes=[R_eacs.tt])
            k.op("act", lambda e: e.activation(out=R_ecs.ap((0, HN), (0, NCH)),
                 in_=R_csum.ap((0, HN), (0, NCH)),
                 func=AF.Exp), reads=[R_csum.tt], writes=[R_ecs.tt])
            k.ops("pe", [
                (lambda e, h=h: e.matmul(psum[3][0:HN, h * NCH:(h + 1) * NCH],
                    R_sel64.ap((0, HN), (h * 64, h * 64 + 32)),
                    R_ecs.ap((0, HN), (0, NCH)),
                    start=True, stop=True))
                for h in range(HN)],
                reads=[R_sel64.tt, R_ecs.tt], writes=[ptt[3]])
            k.op("dve", lambda e: e.tensor_copy(R_ebc.ap((0, HN), (0, HN * NCH)),
                 psum[3][0:HN, 0:HN * NCH]),
                 reads=[ptt[3]], writes=[R_ebc.tt])
            for cp in range(NCP):
                pp = cp % 2
                cc = cp * 128
                k.ops("pe", [lambda e, cc=cc: e.transpose(psum[0][:, 0:32],
                             R_acs.ap((0, 32), (cc, cc + 128)),
                             R_id.ap((0, 32), (0, 32)))],
                      reads=[R_acs.tt, R_id.tt], writes=[ptt[0]])
                k.ops("pe", [lambda e, cc=cc: e.transpose(psum[0][:, 32:64],
                             R_dec.ap((0, 32), (cc, cc + 128)),
                             R_id.ap((0, 32), (0, 32)))],
                      reads=[R_dec.tt, R_id.tt], writes=[ptt[0]])
                k.ops("pe", [lambda e, cc=cc: e.transpose(psum[0][:, 64:96],
                             R_dt.ap((0, 32), (cc, cc + 128)),
                             R_id.ap((64, 96), (64, 96)))],
                      reads=[R_dt.tt, R_id.tt], writes=[ptt[0]])
                k.ops("pe", [lambda e, cc=cc: e.transpose(psum[0][:, 96:128],
                             R_ropB.ap((0, 32), (cc, cc + 128)),
                             R_id.ap((64, 96), (64, 96)))],
                      reads=[R_ropB.tt, R_id.tt], writes=[ptt[0]])
                k.op("dve", lambda e, pp=pp: e.tensor_scalar_mul(R_nacs[pp].ap(),
                     psum[0][:, 0:32], -1.0), reads=[ptt[0]], writes=[R_nacs[pp].tt])
                k.op("dve", lambda e, pp=pp: e.tensor_copy(R_dectk[pp].ap(),
                     psum[0][:, 32:64]), reads=[ptt[0]], writes=[R_dectk[pp].tt])
                k.op("dve", lambda e, pp=pp: e.tensor_copy(R_dttk[pp].ap(),
                     psum[0][:, 64:96]), reads=[ptt[0]], writes=[R_dttk[pp].tt])
                k.op("dve", lambda e, pp=pp: e.tensor_copy(R_bktk[pp].ap(),
                     psum[0][:, 96:128]), reads=[ptt[0]], writes=[R_bktk[pp].tt])
                k.op("dve", lambda e, pp=pp: e.tensor_mul(R_dectk[pp].ap(),
                     R_dectk[pp].ap(), R_dttk[pp].ap()),
                     reads=[R_dectk[pp].tt, R_dttk[pp].tt], writes=[R_dectk[pp].tt])
                k.op("dve", lambda e, pp=pp: e.tensor_tensor(
                    R_bka.ap().rearrange("p (h n) -> p h n", n=32),
                    bass.AP(R_bktk[pp].t, R_bktk[pp].c0,
                            [[R_bktk[pp].t.shape[1], 128], [0, HN], [1, 32]]),
                    bass.AP(R_dectk[pp].t, R_dectk[pp].c0,
                            [[R_dectk[pp].t.shape[1], 128], [1, HN], [0, 32]]),
                    ALU.mult),
                    reads=[R_bktk[pp].tt, R_dectk[pp].tt], writes=[R_bka.tt])
                k.ops("pe", [lambda e, cc=cc: e.matmul(psum[5][0:64, 0:64],
                             R_ropB.ap((0, 32), (cc, cc + 64)),
                             R_ropC.ap((0, 32), (cc, cc + 64)), start=True, stop=True,
                             skip_group_check=True)],
                      reads=[R_ropB.tt, R_ropC.tt], writes=[ptt[5]])
                k.ops("pe", [lambda e, cc=cc: e.matmul(psum[5][64:128, 0:64],
                             R_ropB.ap((0, 32), (cc + 64, cc + 128)),
                             R_ropC.ap((0, 32), (cc + 64, cc + 128)), start=True,
                             stop=True, skip_group_check=True,
                             tile_position=(64, 64))],
                      reads=[R_ropB.tt, R_ropC.tt], writes=[ptt[5]])
                k.op("dve", lambda e, pp=pp: e.tensor_tensor(R_scm[pp].ap(), R_ut.ap(),
                     psum[5][:, 0:64], ALU.mult), reads=[R_ut.tt, ptt[5]],
                     writes=[R_scm[pp].tt])
                def m_front(h):
                    rb_ = 5 + (h % 2)
                    k.ops("pe", [
                        (lambda e, h=h, cc=cc, rb_=rb_: e.matmul(
                            psum[rb_][0:64, 0:64],
                            R_sel64.ap((0, HN), (h * 64, (h + 1) * 64)),
                            R_acs.ap((0, 32), (cc, cc + 64)), start=True, stop=True,
                            skip_group_check=True)),
                        (lambda e, h=h, cc=cc, rb_=rb_: e.matmul(
                            psum[rb_][64:128, 0:64],
                            R_sel64.ap((0, HN), (h * 64, (h + 1) * 64)),
                            R_acs.ap((0, 32), (cc + 64, cc + 128)), start=True,
                            stop=True, skip_group_check=True, tile_position=(0, 64)))],
                        reads=[R_sel64.tt, R_acs.tt], writes=[ptt[rb_]])
                    k.op("dve", lambda e, pp=pp, h=h, rb_=rb_: e.tensor_scalar(
                        R_mdf[h % 2].ap(), psum[rb_][:, 0:64],
                        R_nacs[pp].ap((0, 128), (h, h + 1)), 0.0, ALU.add, ALU.min),
                        reads=[ptt[rb_], R_nacs[pp].tt],
                        writes=[R_mdf[h % 2].tt])
                    k.op("act", lambda e, h=h: e.activation(
                        out=R_mex[h % 4].ap(),
                        in_=R_mdf[h % 2].ap(), func=AF.Exp),
                        reads=[R_mdf[h % 2].tt], writes=[R_mex[h % 4].tt])

                def m_back(h):
                    k.op("dve", lambda e, pp=pp, h=h: e.scalar_tensor_tensor(
                        R_mta.ap((0, 128), (h * 64, (h + 1) * 64)),
                        R_mex[h % 4].ap(),
                        R_dttk[pp].ap((0, 128), (h, h + 1)), R_scm[pp].ap(),
                        ALU.mult, ALU.mult),
                        reads=[R_mex[h % 4].tt, R_dttk[pp].tt, R_scm[pp].tt],
                        writes=[R_mta.tt])
                m_front(0)
                m_front(1)
                for h in range(HN):
                    if h + 2 < HN:
                        m_front(h + 2)
                    m_back(h)
                for hp in range(HN // 2):
                    for hh in range(2):
                        h = hp * 2 + hh
                        ob = hh * 64
                        xsl0 = lambda h=h, cp=cp: R_xst.ap(
                            (0, 64), (cp * DI_ + h * 64, cp * DI_ + (h + 1) * 64))
                        xsl1 = lambda h=h, cp=cp: R_xst.ap(
                            (64, 128), (cp * DI_ + h * 64, cp * DI_ + (h + 1) * 64))
                        k.ops("pe", [
                            (lambda e, h=h, ob=ob, xsl0=xsl0: e.matmul(
                                psum[1][ob:ob + 64, 0:64], xsl0(),
                                R_mta.ap((0, 64), (h * 64, (h + 1) * 64)),
                                start=True, stop=False, skip_group_check=True,
                                tile_position=(0, ob)))],
                            reads=[R_xst.tt, R_mta.tt], writes=[ptt[1]])
                        k.ops("pe", [
                            (lambda e, h=h, ob=ob, xsl1=xsl1: e.matmul(
                                psum[1][ob:ob + 64, 64:128], xsl1(),
                                R_mta.ap((64, 128), (h * 64, (h + 1) * 64)),
                                start=False, stop=False, skip_group_check=True,
                                tile_position=(64, ob)))],
                            reads=[R_xst.tt, R_mta.tt], writes=[ptt[1]])
                        k.ops("pe", [
                            (lambda e, h=h, ob=ob, xsl0=xsl0: e.matmul(
                                psum[1][ob:ob + 64, 0:64], xsl0(),
                                R_dident.ap((0, 64), (h * 64, (h + 1) * 64)),
                                start=False, stop=False, skip_group_check=True,
                                tile_position=(0, ob)))],
                            reads=[R_xst.tt, R_dident.tt], writes=[ptt[1]])
                        k.ops("pe", [
                            (lambda e, h=h, ob=ob, xsl1=xsl1: e.matmul(
                                psum[1][ob:ob + 64, 64:128], xsl1(),
                                R_dident.ap((64, 128), (h * 64, (h + 1) * 64)),
                                start=False, stop=False, skip_group_check=True,
                                tile_position=(64, ob)))],
                            reads=[R_xst.tt, R_dident.tt], writes=[ptt[1]])
                        k.ops("pe", [
                            (lambda e, h=h, xsl0=xsl0: e.matmul(
                                psum[2][0:32, 0:64],
                                R_bka.ap((0, 64), (h * 32, (h + 1) * 32)),
                                xsl0(), start=True, stop=True, skip_group_check=True))],
                            reads=[R_bka.tt, R_xst.tt], writes=[ptt[2]])
                        k.ops("pe", [
                            (lambda e, h=h, xsl1=xsl1: e.matmul(
                                psum[2][32:64, 0:64],
                                R_bka.ap((64, 128), (h * 32, (h + 1) * 32)),
                                xsl1(), start=True, stop=True, skip_group_check=True,
                                tile_position=(64, 32)))],
                            reads=[R_bka.tt, R_xst.tt], writes=[ptt[2]])
                        for ci in range(2):
                            k.op("dve", lambda e, hh=hh, ci=ci: e.tensor_copy(
                                R_stsb[hh][ci].ap(),
                                psum[2][ci * 32:(ci + 1) * 32, 0:64]),
                                reads=[ptt[2]], writes=[R_stsb[hh][ci].tt])
                        k.ops("pe", [
                            (lambda e, h=h, cc=cc: e.matmul(psum[3][0:HN, 0:128],
                                R_sel64.ap((0, HN), (h * 64, h * 64 + 32)),
                                R_eacs.ap((0, 32), (cc, cc + 128)), start=True,
                                stop=True, skip_group_check=True))],
                            reads=[R_sel64.tt, R_eacs.tt], writes=[ptt[3]])
                        k.op("dve", lambda e, cc=cc, h=h: e.tensor_tensor(
                            R_cks[h % 2].ap(), R_ropC.ap((0, 32), (cc, cc + 128)),
                            psum[3][0:HN, 0:128], ALU.mult),
                            reads=[R_ropC.tt, ptt[3]], writes=[R_cks[h % 2].tt])
                        csl0 = cp * 2
                        k.op("dve", lambda e, h=h, hh=hh, csl0=csl0: e.tensor_scalar(
                            R_cke[hh].ap(), R_cks[h % 2].ap((0, HN), (64, 128)),
                            R_ebc.ap((0, HN), (h * NCH + csl0, h * NCH + csl0 + 1)),
                            None, ALU.mult),
                            reads=[R_cks[h % 2].tt, R_ebc.tt], writes=[R_cke[hh].tt])
                        k.ops("pe", [
                            (lambda e, h=h, ob=ob: e.matmul(
                                psum[1][ob:ob + 64, 0:64],
                                R_sal[h].ap(),
                                R_cks[h % 2].ap((0, HN), (0, 64)),
                                start=False, stop=False,
                                skip_group_check=True, tile_position=(32, ob)))],
                            reads=[R_sal[h].tt, R_cks[h % 2].tt], writes=[ptt[1]])
                        k.ops("pe", [
                            (lambda e, h=h, ob=ob, hh=hh: e.matmul(
                                psum[1][ob:ob + 64, 64:128],
                                R_sal[h].ap(), R_cke[hh].ap(),
                                start=False, stop=False,
                                skip_group_check=True, tile_position=(32, ob)))],
                            reads=[R_sal[h].tt, R_cke[hh].tt], writes=[ptt[1]])
                        k.ops("pe", [
                            (lambda e, h=h, ob=ob, hh=hh: e.matmul(
                                psum[1][ob:ob + 64, 64:128],
                                R_stsb[hh][0].ap(),
                                R_cks[h % 2].ap((0, HN), (64, 128)),
                                start=False, stop=(hh == 1),
                                skip_group_check=True, tile_position=(32, ob)))],
                            reads=[R_stsb[hh][0].tt, R_cks[h % 2].tt], writes=[ptt[1]])
                        for ci in range(2):
                            csl = cp * 2 + ci
                            k.op("dve", lambda e, h=h, ci=ci, csl=csl, hh=hh:
                                 e.scalar_tensor_tensor(
                                    R_sal[h].ap(), R_sal[h].ap(),
                                    R_ebc.ap((0, HN), (h * NCH + csl, h * NCH + csl + 1)),
                                    R_stsb[hh][ci].ap(),
                                    ALU.mult, ALU.add),
                                 reads=[R_sal[h].tt, R_ebc.tt, R_stsb[hh][ci].tt],
                                 writes=[R_sal[h].tt])
                    k.op("dve", lambda e, hp=hp, cp=cp: e.tensor_copy(
                        R_y.ap((0, 128), (hp * SL + cp * 128, hp * SL + (cp + 1) * 128)),
                        psum[1][:, 0:128]), reads=[ptt[1]], writes=[R_y.tt])
            if dev:
                ch_ydbg = k.chan("ch_ydbg")
                for fc in range(16):
                    k.dma("pool", ch_ydbg, y_dbg[b, fc * 128:(fc + 1) * 128, u0:u0 + SL],
                          R_y.ap((0, 128), (fc * SL, (fc + 1) * SL)), reads=[R_y.tt])
            R_sq = SM.carve("lnsq", 128, 512, c0=0)
            rstd = SM.carve("ln_rstd", 1, 512, c0=1024)
            msq = SM.carve("ln_msq", 1, 512, c0=1536)
            var = SM.carve("ln_var", 1, 512, c0=2048)
            for fc in range(16):
                k.op("dve", lambda e, fc=fc: e.tensor_mul(
                    R_yg.ap((0, 128), (fc * SL, (fc + 1) * SL)),
                    R_y.ap((0, 128), (fc * SL, (fc + 1) * SL)),
                    R_sz.ap((0, 128), (fc * SL, (fc + 1) * SL))),
                    reads=[R_y.tt, R_sz.tt], writes=[R_yg.tt])
                k.op("act", lambda e, fc=fc: e.activation(out=R_sq.ap((0, 128), (0, SL)),
                     in_=R_yg.ap((0, 128), (fc * SL, (fc + 1) * SL)), func=AF.Square),
                     reads=[R_yg.tt], writes=[R_sq.tt])
                k.ops("pe", [lambda e, fc=fc: e.matmul(psum[6][0:1, 0:SL],
                             R_onec.ap(), R_sq.ap((0, 128), (0, SL)),
                             start=(fc == 0), stop=(fc == 15))],
                      reads=[R_onec.tt, R_sq.tt], writes=[ptt[6]])
            k.op("dve", lambda e: e.tensor_scalar_mul(var.ap((0, 1), (0, SL)),
                 psum[6][0:1, 0:SL], 1.0 / DI_), reads=[ptt[6]], writes=[var.tt])
            k.op("dve", lambda e: e.tensor_scalar_add(var.ap((0, 1), (0, SL)),
                 var.ap((0, 1), (0, SL)), 1e-6), reads=[var.tt], writes=[var.tt])
            k.op("act", lambda e: e.activation(out=msq.ap((0, 1), (0, SL)),
                 in_=var.ap((0, 1), (0, SL)), func=AF.Ln),
                 reads=[var.tt], writes=[msq.tt])
            k.op("act", lambda e: e.activation(out=rstd.ap((0, 1), (0, SL)),
                 in_=msq.ap((0, 1), (0, SL)), func=AF.Exp, scale=-0.5),
                 reads=[msq.tt], writes=[rstd.tt])
            k.ops("pe", [lambda e: e.matmul(psum[7][0:128, 0:SL], R_ones.ap(),
                         rstd.ap((0, 1), (0, SL)), start=True, stop=True)],
                  reads=[R_ones.tt, rstd.tt], writes=[ptt[7]])
            for fc in range(16):
                k.op("dve", lambda e, fc=fc: e.tensor_mul(
                    R_yg.ap((0, 128), (fc * SL, (fc + 1) * SL)),
                    R_yg.ap((0, 128), (fc * SL, (fc + 1) * SL)), psum[7][0:128, 0:SL]),
                    reads=[R_yg.tt, ptt[7]], writes=[R_yg.tt])
                k.op("dve", lambda e, fc=fc: e.tensor_scalar(
                    R_y.ap((0, 128), (fc * SL, (fc + 1) * SL)),
                    R_yg.ap((0, 128), (fc * SL, (fc + 1) * SL)),
                    R_rmsw.ap((0, 128), (fc, fc + 1)), None, ALU.mult),
                    reads=[R_yg.tt, R_rmsw.tt], writes=[R_y.tt])
            for dgrp in range(2):
                for kh in range(2):
                    osl = wslots(512, [(w1_out[kh * 1024:(kh + 1) * 1024,
                                               dgrp * 512:(dgrp + 1) * 512], 8)])
                    for mo in range(4):
                        k.ops("pe", [
                            (lambda e, kk=kk, kh=kh, mo=mo, osl=osl: e.matmul(
                                psum[mo][:, 0:SL],
                                osl[kk].ap((0, 128), (mo * 128, (mo + 1) * 128)),
                                R_y.ap((0, 128), ((kh * 8 + kk) * SL, (kh * 8 + kk + 1) * SL)),
                                start=(kh == 0 and kk == 0), stop=(kh == 1 and kk == 7)))
                            for kk in range(8)],
                            reads=[s.tt for s in osl] + [R_y.tt], writes=[ptt[mo]])
                for mo in range(4):
                    dmo = dgrp * 4 + mo
                    k.op("dve", lambda e, mo=mo, dmo=dmo, u0=u0: e.tensor_add(
                        R_yg.ap((0, 128), (mo * SL, (mo + 1) * SL)), psum[mo][:, 0:SL],
                        xnap(dmo, u0, u0 + SL)),
                        reads=[ptt[mo], R_xn.tt], writes=[R_yg.tt])
                    k.dma("sp", ch_s1, s1_d[b, dmo * 128:(dmo + 1) * 128, u0:u0 + SL],
                          R_yg.ap((0, 128), (mo * SL, (mo + 1) * SL)), reads=[R_yg.tt])
        tt_s1 = TT("s1_dram")
        tt_s1.w["ch_s1"] = ch_s1.cnt
        return tt_s1

    def moe_fin(b, tt_x1, tt_s1, ch_out):
        R_xn = XN.live["xn"]
        xnap = lambda c, t0, t1: R_xn.ap((0, 128), (c * T + t0, c * T + t1))
        J5 = min(512, TM)
        for tm in range(T // TM):
            u0 = tm * TM
            for A in (AB, CB, WB, FA, SM):
                A.new_phase()
            R_ptok = SM.carve("ptok", 128, 8 * (TM // 128), c0=2560)
            R_pfm = SM.carve("pfm", NE, TM, r0=64, c0=1536)
            R_lgt = SM.carve("lgt", NE, 1024, c0=2688)
            R_m1 = [SM.carve(f"m1_{pp}", 128, 1, c0=3712 + pp) for pp in range(2)]
            R_m2 = [SM.carve(f"m2_{pp}", 128, 1, c0=3714 + pp) for pp in range(2)]
            R_msk = [SM.carve(f"msk{pp}", 128, 8, c0=3720 + pp * 8) for pp in range(2)]
            R_exp = [SM.carve(f"mexp{pp}", 128, 8, c0=3736 + pp * 8) for pp in range(2)]
            R_sum = [SM.carve(f"msum{pp}", 128, 1, c0=3752 + pp) for pp in range(2)]
            R_e2 = [SM.carve(f"me2_{pp}", 128, 1, c0=3754 + pp) for pp in range(2)]
            R_den = [SM.carve(f"mden{pp}", 128, 1, c0=3756 + pp) for pp in range(2)]
            R_he = AB.carve("he", 128, 16 * TM, c0=0)
            R_g1 = CB.carve("g1fm", 128, 8 * TM, c0=0)
            R_gacc = FA.carve("gacc", 128, 8 * TM, c0=0)
            for j in range(TM // J5):
                t0 = u0 + j * J5
                k.ops("pe", [
                    (lambda e, kk=kk, t0=t0: e.matmul(psum[0][0:NE, 0:J5],
                        R_rtrb.ap((0, 128), (kk * 8, (kk + 1) * 8)),
                        xnap(kk, t0, t0 + J5), start=(kk == 0), stop=(kk == 7)))
                    for kk in range(8)],
                    reads=[R_rtrb.tt, R_xn.tt], writes=[ptt[0]])
                jo = (j % 2) * J5
                k.op("dve", lambda e, jo=jo: e.tensor_copy(
                     R_lgt.ap((0, NE), (jo, jo + J5)), psum[0][0:NE, 0:J5]),
                     reads=[ptt[0]], writes=[R_lgt.tt])
                def emit_top2(tc4, tcg, jo):
                    pp = tcg % 2
                    tb_ = 6 + pp
                    m1, m2, msk = R_m1[pp], R_m2[pp], R_msk[pp]
                    mex, msum, me2, mden = R_exp[pp], R_sum[pp], R_e2[pp], R_den[pp]
                    k.ops("pe", [lambda e, tc4=tc4, jo=jo, tb_=tb_: e.transpose(
                                 psum[tb_][:, 0:NE],
                                 R_lgt.ap((0, NE), (jo + tc4 * 128, jo + (tc4 + 1) * 128)),
                                 R_id.ap((0, NE), (0, NE)))],
                          reads=[R_lgt.tt, R_id.tt], writes=[ptt[tb_]])
                    lg = lambda tb_=tb_: psum[tb_][:, 0:NE]
                    k.op("dve", lambda e: e.reduce_max(m1.ap(), lg(), axis=AX.X),
                         reads=[ptt[tb_]], writes=[m1.tt])
                    k.op("dve", lambda e: e.tensor_scalar(msk.ap(), lg(),
                         m1.ap(), None, ALU.is_equal),
                         reads=[ptt[tb_], m1.tt], writes=[msk.tt])
                    k.op("dve", lambda e: e.scalar_tensor_tensor(msk.ap(), msk.ap(),
                         -1e30, lg(), ALU.mult, ALU.add),
                         reads=[msk.tt, ptt[tb_]], writes=[msk.tt])
                    k.op("dve", lambda e: e.reduce_max(m2.ap(), msk.ap(), axis=AX.X),
                         reads=[msk.tt], writes=[m2.tt])
                    k.op("dve", lambda e: e.tensor_scalar_mul(me2.ap(), m1.ap(), -1.0),
                         reads=[m1.tt], writes=[me2.tt])
                    k.op("act", lambda e: e.activation(out=mex.ap(), in_=lg(),
                         func=AF.Exp, bias=me2.ap()),
                         reads=[ptt[tb_], me2.tt], writes=[mex.tt])
                    k.op("dve", lambda e: e.reduce_sum(msum.ap(), mex.ap(), axis=AX.X),
                         reads=[mex.tt], writes=[msum.tt])
                    k.op("dve", lambda e: e.tensor_sub(mden.ap(), m2.ap(), m1.ap()),
                         reads=[m2.tt, m1.tt], writes=[mden.tt])
                    k.op("act", lambda e: e.activation(out=me2.ap(), in_=mden.ap(),
                         func=AF.Exp), reads=[mden.tt], writes=[me2.tt])
                    k.op("dve", lambda e: e.scalar_tensor_tensor(mden.ap(), msum.ap(),
                         1e-9, me2.ap(), ALU.mult, ALU.add),
                         reads=[msum.tt, me2.tt], writes=[mden.tt])
                    k.op("dve", lambda e: e.tensor_scalar_add(mden.ap(), mden.ap(), 1.0),
                         reads=[mden.tt], writes=[mden.tt])
                    k.op("dve", lambda e: e.reciprocal(mden.ap(), mden.ap()),
                         reads=[mden.tt], writes=[mden.tt])
                    k.op("dve", lambda e: e.tensor_scalar(msk.ap(), lg(),
                         m2.ap(), None, ALU.is_ge),
                         reads=[ptt[tb_], m2.tt], writes=[msk.tt])
                    k.op("dve", lambda e: e.tensor_mul(mex.ap(), mex.ap(), msk.ap()),
                         reads=[mex.tt, msk.tt], writes=[mex.tt])
                    k.op("dve", lambda e, tcg=tcg: e.tensor_scalar(
                         R_ptok.ap((0, 128), (tcg * 8, (tcg + 1) * 8)), mex.ap(),
                         mden.ap(), None, ALU.mult),
                         reads=[mex.tt, mden.tt], writes=[R_ptok.tt])
                for tc4 in range(J5 // 128):
                    emit_top2(tc4, j * (J5 // 128) + tc4, jo)
            for tc4 in range(TM // J5):
                nt4 = J5 // 128
                k.ops("pe", [
                    (lambda e, tc=tc, tc4=tc4, nt4=nt4: e.transpose(
                        psum[2][0:NE, tc * 128:(tc + 1) * 128],
                        R_ptok.ap((0, 128), ((tc4 * nt4 + tc) * 8,
                                             (tc4 * nt4 + tc + 1) * 8)),
                        R_id.ap()))
                    for tc in range(nt4)],
                    reads=[R_ptok.tt, R_id.tt], writes=[ptt[2]])
                k.op("dve", lambda e, tc4=tc4: e.tensor_copy(
                    R_pfm.ap((0, NE), (tc4 * J5, (tc4 + 1) * J5)),
                    psum[2][0:NE, 0:J5]),
                    reads=[ptt[2]], writes=[R_pfm.tt])
            for ei in range(NE):
                for fg in range(4):
                    gsl = wslots(512, [(mgw[ei, :, fg * 512:(fg + 1) * 512], 8)])
                    for j in range(TM // J5):
                        for mi in range(4):
                            m = fg * 4 + mi
                            bank = (fg * 4 + mi) % 6
                            k.ops("pe", [
                                (lambda e, kk=kk, mi=mi, j=j, bank=bank, gsl=gsl, u0=u0: e.matmul(
                                    psum[bank][:, 0:J5],
                                    gsl[kk].ap((0, 128), (mi * 128, (mi + 1) * 128)),
                                    xnap(kk, u0 + j * J5, u0 + (j + 1) * J5),
                                    start=(kk == 0), stop=(kk == 7)))
                                for kk in range(8)],
                                reads=[s.tt for s in gsl] + [R_xn.tt], writes=[ptt[bank]])
                            k.op("act", lambda e, m=m, j=j, bank=bank, ei=ei: e.activation(
                                out=R_he.ap((0, 128), (m * TM + j * J5,
                                                       m * TM + (j + 1) * J5)),
                                in_=psum[bank][:, 0:J5], func=AF.Silu,
                                bias=R_mgb.ap((0, 128), (ei * 16 + m, ei * 16 + m + 1))),
                                reads=[ptt[bank], R_mgb.tt], writes=[R_he.tt])
                for fg in range(4):
                    usl = wslots(512, [(muw[ei, :, fg * 512:(fg + 1) * 512], 8)])
                    for j in range(TM // J5):
                        for mi in range(4):
                            m = fg * 4 + mi
                            bank = (fg * 4 + mi) % 6
                            k.ops("pe", [
                                (lambda e, kk=kk, mi=mi, j=j, bank=bank, usl=usl, u0=u0: e.matmul(
                                    psum[bank][:, 0:J5],
                                    usl[kk].ap((0, 128), (mi * 128, (mi + 1) * 128)),
                                    xnap(kk, u0 + j * J5, u0 + (j + 1) * J5),
                                    start=(kk == 0), stop=(kk == 7)))
                                for kk in range(8)],
                                reads=[s.tt for s in usl] + [R_xn.tt], writes=[ptt[bank]])
                            k.op("dve", lambda e, m=m, j=j, bank=bank, ei=ei:
                                 e.scalar_tensor_tensor(
                                    R_he.ap((0, 128), (m * TM + j * J5,
                                                       m * TM + (j + 1) * J5)),
                                    psum[bank][:, 0:J5],
                                    R_mub.ap((0, 128), (ei * 16 + m, ei * 16 + m + 1)),
                                    R_he.ap((0, 128), (m * TM + j * J5,
                                                       m * TM + (j + 1) * J5)),
                                    ALU.add, ALU.mult),
                                 reads=[ptt[bank], R_mub.tt, R_he.tt], writes=[R_he.tt])
                for dj in range(2):
                    for kh in range(2):
                        dsl = wslots(512, [(mdw[ei, kh * 1024:(kh + 1) * 1024,
                                                dj * 512:(dj + 1) * 512], 8)])
                        for tc in range(TM // 128):
                            bank = tc % 8
                            k.ops("pe", [
                                (lambda e, kk=kk, kh=kh, tc=tc, bank=bank, dsl=dsl: e.matmul(
                                    psum[bank][:, 0:512],
                                    R_he.ap((0, 128), ((kh * 8 + kk) * TM + tc * 128,
                                                       (kh * 8 + kk) * TM + (tc + 1) * 128)),
                                    dsl[kk].ap(), start=(kh == 0 and kk == 0),
                                    stop=(kh == 1 and kk == 7)))
                                for kk in range(8)],
                                reads=[s.tt for s in dsl] + [R_he.tt], writes=[ptt[bank]])
                    for tc in range(TM // 128):
                        bank = tc % 8
                        gap = lambda tc=tc, dj=dj: R_gacc.ap(
                            (0, 128), (tc * 1024 + dj * 512, tc * 1024 + (dj + 1) * 512))
                        pcol = lambda tc=tc, ei=ei: R_ptok.ap((0, 128),
                            (tc * 8 + ei, tc * 8 + ei + 1))
                        if ei == 0:
                            k.op("dve", lambda e, bank=bank, gap=gap, pcol=pcol:
                                 e.tensor_scalar(gap(), psum[bank][:, 0:512], pcol(),
                                                 None, ALU.mult),
                                 reads=[ptt[bank], R_ptok.tt], writes=[R_gacc.tt])
                        else:
                            k.op("dve", lambda e, bank=bank, gap=gap, pcol=pcol:
                                 e.scalar_tensor_tensor(gap(), psum[bank][:, 0:512],
                                                        pcol(), gap(), ALU.mult, ALU.add),
                                 reads=[ptt[bank], R_ptok.tt, R_gacc.tt],
                                 writes=[R_gacc.tt])
            for tc in range(TM // 128):
                for dj in range(2):
                    bank = (tc * 2 + dj) % 8
                    k.ops("pe", [
                        (lambda e, tc=tc, dj=dj, bank=bank: e.matmul(
                            psum[bank][:, 0:512],
                            R_pfm.ap((0, NE), (tc * 128, (tc + 1) * 128)),
                            R_mdbblk.ap((0, NE), (dj * 512, (dj + 1) * 512)),
                            start=True, stop=True))],
                        reads=[R_pfm.tt, R_mdbblk.tt], writes=[ptt[bank]])
                    k.op("dve", lambda e, tc=tc, dj=dj, bank=bank: e.tensor_add(
                        R_gacc.ap((0, 128), (tc * 1024 + dj * 512,
                                             tc * 1024 + (dj + 1) * 512)),
                        R_gacc.ap((0, 128), (tc * 1024 + dj * 512,
                                             tc * 1024 + (dj + 1) * 512)),
                        psum[bank][:, 0:512]),
                        reads=[R_gacc.tt, ptt[bank]], writes=[R_gacc.tt])
            for tc in range(TM // 128):
                for c in range(8):
                    bank = c % 2
                    k.ops("pe", [lambda e, tc=tc, c=c, bank=bank: e.transpose(
                        psum[bank][:, 0:128],
                        R_gacc.ap((0, 128), (tc * 1024 + c * 128,
                                             tc * 1024 + (c + 1) * 128)),
                        R_id.ap())],
                        reads=[R_gacc.tt, R_id.tt], writes=[ptt[bank]])
                    k.op("act", lambda e, tc=tc, c=c, bank=bank: e.activation(
                        out=R_g1.ap((0, 128), (c * TM + tc * 128, c * TM + (tc + 1) * 128)),
                        in_=psum[bank][:, 0:128], func=AF.Silu),
                        reads=[ptt[bank]], writes=[R_g1.tt])
            if dev:
                ch_g1d = k.chan(f"chg{b}_{tm}")
                for tc in range(TM // 128):
                    k.dma("pool", ch_g1d,
                          ga_dbg[b, u0 + tc * 128:u0 + (tc + 1) * 128, :],
                          R_gacc.ap((0, 128), (tc * 1024, (tc + 1) * 1024)),
                          reads=[R_gacc.tt])
                for c in range(8):
                    k.dma("pool", ch_g1d, g1_dbg[b, c * 128:(c + 1) * 128, u0:u0 + TM],
                          R_g1.ap((0, 128), (c * TM, (c + 1) * TM)), reads=[R_g1.tt])
                for tc in range(TM // 128):
                    k.dma("pool", ch_g1d,
                          p_dbg[b, u0 + tc * 128:u0 + (tc + 1) * 128, :],
                          R_ptok.ap((0, 128), (tc * 8, (tc + 1) * 8)),
                          reads=[R_ptok.tt])
                for _tt in (R_gacc.tt, R_g1.tt, R_ptok.tt):
                    _tt.readers[ch_g1d.name] = ch_g1d.cnt
            FA.new_phase()
            R_x1 = FA.carve("fx1", 128, 8 * J5, c0=0)
            R_s1 = FA.carve("fs1", 128, 8 * J5, c0=8 * J5)
            ch_f = k.chan("ch_f")
            for j in range(TM // J5):
                t0 = u0 + j * J5
                k.dma("sp", ch_f, R_x1.ap().rearrange("p (c t) -> p c t", c=8),
                      x1_d[b, :, t0:t0 + J5].rearrange("(c p) t -> p c t", p=128),
                      reads=[tt_x1], writes=[R_x1.tt])
                k.dma("sp", k.chan("ch_fs"), R_s1.ap().rearrange("p (c t) -> p c t", c=8),
                      s1_d[b, :, t0:t0 + J5].rearrange("(c p) t -> p c t", p=128),
                      reads=[tt_s1], writes=[R_s1.tt])
                for c in range(8):
                    k.op("dve", lambda e, c=c, j=j: e.tensor_mul(
                        R_s1.ap((0, 128), (c * J5, (c + 1) * J5)),
                        R_s1.ap((0, 128), (c * J5, (c + 1) * J5)),
                        R_g1.ap((0, 128), (c * TM + j * J5, c * TM + (j + 1) * J5))),
                        reads=[R_s1.tt, R_g1.tt], writes=[R_s1.tt])
                    k.op("dve", lambda e, c=c: e.tensor_add(
                        R_x1.ap((0, 128), (c * J5, (c + 1) * J5)),
                        R_x1.ap((0, 128), (c * J5, (c + 1) * J5)),
                        R_s1.ap((0, 128), (c * J5, (c + 1) * J5))),
                        reads=[R_x1.tt, R_s1.tt], writes=[R_x1.tt])
                for tb in range(J5 // 128):
                    stage = FA.alias(R_s1, 0, 128, tb * 1024, 1024)
                    for c in range(8):
                        bank = c % 2
                        k.ops("pe", [lambda e, tb=tb, c=c, bank=bank: e.transpose(
                            psum[bank][:, 0:128],
                            R_x1.ap((0, 128), (c * J5 + tb * 128,
                                               c * J5 + (tb + 1) * 128)),
                            R_id.ap())],
                            reads=[R_x1.tt, R_id.tt], writes=[ptt[bank]])
                        k.op("dve", lambda e, c=c, bank=bank, stage=stage: e.tensor_copy(
                            stage.ap((0, 128), (c * 128, (c + 1) * 128)),
                            psum[bank][:, 0:128]),
                            reads=[ptt[bank]], writes=[stage.tt])
                    k.dma("sp", ch_out, out[b, t0 + tb * 128:t0 + (tb + 1) * 128, :],
                          stage.ap(), reads=[stage.tt])

    ch_out = k.chan("ch_out")
    for b in range(NB):
        tt_x1 = l0(b)
        tt_s1 = None
        if stages >= 2:
            tt_s1 = l1a(b)
        if stages >= 3:
            moe_fin(b, tt_x1, tt_s1, ch_out)
    for chn, chan in k.chans.items():
        t = TT("fin_" + chn)
        t.w[chn] = chan.cnt
        k.barrier_wait("pool", [t])
    k.op("pool", lambda e: e.engine_nop())

    with nc.Block() as block:
        k.build(block)
    stack.close()
    return nc, k



def kernel(**inputs):
    sh, per_core = _prep_inputs(inputs)
    nc, _ = build_core(dev=False, stages=3, Tdev=2048)
    in_maps = []
    for c in range(NCORES):
        m = dict(sh)
        m["x"] = per_core[c]
        in_maps.append(m)
    res = run_bass_kernel_spmd(nc, in_maps, core_ids=list(range(NCORES)))
    outs = [res.results[c]["out"] for c in range(NCORES)]
    return np.ascontiguousarray(np.concatenate(outs, axis=0))
